# version 20
# baseline (speedup 1.0000x reference)
import numpy as np
import ml_dtypes
import concourse.bass as bass
import concourse.mybir as mybir
from concourse.bass_utils import run_bass_kernel_spmd
from contextlib import ExitStack

F32 = mybir.dt.float32
BF16 = mybir.dt.bfloat16
U8 = mybir.dt.uint8
U16 = mybir.dt.uint16
F8 = mybir.dt.float8e4
AX = mybir.AxisListType.X
OP = mybir.AluOpType
ACT = mybir.ActivationFunctionType

P = 128
EPS_LN = 1.0e-5
FP8_MAX = 224.0


class Ctr:
    def __init__(self, sem, step=1):
        self.sem = sem
        self.step = step
        self.n = 0

    def inc(self, inst, k=1):
        inst.then_inc(self.sem, self.step * k)
        self.n += self.step * k
        return self.n

    def wait(self, e, n=None):
        n = self.n if n is None else n
        if n > 0:
            e.wait_ge(self.sem, n)


class Streams:
    def __init__(self, nc, es):
        self.nc = nc
        self.es = es
        self.m = {}

    def get(self, name, bi=0):
        key = (name, bi)
        if key not in self.m:
            self.m[key] = Ctr(
                self.es.enter_context(self.nc.semaphore(f"dm_{name}_{bi}")), 16)
        return self.m[key]

    def put(self, e, name, bi, out, in_):
        c = self.get(name, bi)
        c.inc(e.dma_start(out=out, in_=in_))
        return c

    def all(self):
        return list(self.m.values())


def build_program(TK=16, DC=16, debug=False):
    T = TK * P
    D = DC * P
    TW = min(512, T)
    NTT = T // TW
    IT = TW // P
    OW = min(512, D)
    NOT_ = D // OW

    nc = bass.Bass("TRN2", target_bir_lowering=False)

    x_d = nc.dram_tensor("x", [T, D], F32, kind="ExternalInput").ap()
    w_r = {}
    for nm in ("q", "k", "g", "v"):
        w_r[nm] = nc.dram_tensor(f"w{nm}_r", [DC, P, DC * P], F32,
                                 kind="ExternalInput").ap()
    wp_t = nc.dram_tensor("wp_t", [D, D], F32, kind="ExternalInput").ap()
    wf_t = nc.dram_tensor("wf_t", [D, D], F32, kind="ExternalInput").ap()
    ident_d = nc.dram_tensor("ident", [P, P], BF16, kind="ExternalInput").ap()
    trimask_d = nc.dram_tensor("trimask", [P, P], BF16, kind="ExternalInput").ap()
    out_d = nc.dram_tensor("out", [T, D], F32, kind="ExternalOutput").ap()

    sck = "ExternalOutput" if debug else "Internal"
    xh_sc = nc.dram_tensor("xh_sc", [TK, P, D], BF16, kind=sck).ap()
    xl_sc = nc.dram_tensor("xl_sc", [TK, P, D], BF16, kind=sck).ap()
    q_sc = nc.dram_tensor("q_sc", [NTT, DC, P, TW], BF16, kind=sck).ap()
    g_sc = nc.dram_tensor("g_sc", [NTT, DC, P, TW], BF16, kind=sck).ap()
    v_sc = nc.dram_tensor("v_sc", [NTT, DC, P, TW], BF16, kind=sck).ap()
    o_sc = nc.dram_tensor("o_sc", [NTT, DC, P, TW], BF16, kind=sck).ap()
    ffn_sc = nc.dram_tensor("ffn_sc", [TK, P, D], U16, kind=sck).ap()
    wbf_sc = {}
    for nm in ("q", "k", "g", "v"):
        wbf_sc[nm] = nc.dram_tensor(f"wbf_{nm}", [DC, P, DC * P], BF16,
                                    kind="Internal").ap()
    wpbf_sc = nc.dram_tensor("wpbf", [DC, P, D], BF16, kind="Internal").ap()
    wh_sc = nc.dram_tensor("wh_sc", [DC, P, D], BF16, kind="Internal").ap()
    wl_sc = nc.dram_tensor("wl_sc", [DC, P, D], BF16, kind="Internal").ap()
    if debug:
        xn2_dbg = nc.dram_tensor("xn2_dbg", [P, DC * T], BF16,
                                 kind="ExternalOutput").ap()
        kfm_dbg = nc.dram_tensor("kfm_dbg", [P, DC * T], BF16,
                                 kind="ExternalOutput").ap()
        vtm_dbg = nc.dram_tensor("vtm_dbg", [P, TK * D], BF16,
                                 kind="ExternalOutput").ap()

    es = ExitStack()
    ident = nc.alloc_sbuf_tensor("ident_s", [P, P], BF16)
    trimask = nc.alloc_sbuf_tensor("trimask_s", [P, P], BF16)
    c224 = nc.alloc_sbuf_tensor("c224", [P, DC * DC], F32)
    epsk = nc.alloc_sbuf_tensor("epsk", [P, 48], F32)
    epszm = nc.alloc_sbuf_tensor("epszm", [P, 48], F32)
    epst = nc.alloc_sbuf_tensor("epst", [P, 1], F32)
    ps = [nc.alloc_psum_tensor(f"ps{i}", [P, 512], F32) for i in range(8)]

    dm = Streams(nc, es)
    tC = Ctr(es.enter_context(nc.semaphore("tC")))
    vC = Ctr(es.enter_context(nc.semaphore("vC")))
    sC = Ctr(es.enter_context(nc.semaphore("sC")))

    blk = es.enter_context(nc.Block())
    SY, VE, SE, TE, GE = blk.sync, blk.vector, blk.scalar, blk.tensor, blk.gpsimd

    def join(*engines):
        cs = [c for c in ([tC, vC, sC] + dm.all()) if c.n > 0]
        for engf in engines:
            def _f(e, cs=cs):
                for c in cs:
                    e.wait_ge(c.sem, c.n)
            engf(_f)

    @SY
    def _(sy):
        dm.put(sy, "const", 0, ident[:], ident_d[:, :])
        dm.put(sy, "const", 0, trimask[:], trimask_d[:, :])
    s_const = dm.get("const")

    @VE
    def _(v):
        v.memset(c224[:], FP8_MAX)
        vC.inc(v.memset(epsk[:, 0:1], 2.0 ** -16))
        for k in range(1, 48):
            v.wait_ge(vC.sem, vC.n)
            vC.inc(v.tensor_scalar(epsk[:, k:k + 1], epsk[:, k - 1:k], 0.5,
                                   None, OP.mult))
        v.wait_ge(vC.sem, vC.n)
        vC.inc(v.tensor_scalar(epsk[:, :], epsk[:, :], 1.0, -1.0, OP.add, OP.add))
        v.wait_ge(vC.sem, vC.n)
        vC.inc(v.tensor_single_scalar(epszm[:, :], epsk[:, :], 0.0, OP.is_equal))
        v.wait_ge(vC.sem, vC.n)
        vC.inc(v.tensor_tensor(epsk[:, :], epsk[:, :], epszm[:, :], OP.add))
        v.wait_ge(vC.sem, vC.n)
        vC.inc(v.tensor_reduce(epst[:, :], epsk[:, :], AX, OP.min))
    v_const = vC.n

    cast_stage_f = None
    cast_stage_b = None
    cast_busy = [0, 0]

    def make_cast_job(src_ap, dst_ap, sname):
        state = {"ld": 0, "cs": 0}

        def pload():
            i = state["ld"]
            if i >= DC:
                return
            state["ld"] += 1
            bi = i % 2

            @GE
            def _(g, i=i, bi=bi):
                if cast_busy[bi]:
                    g.wait_ge(vC.sem, cast_busy[bi])
                dm.put(g, "cwld", bi, cast_stage_f[bi][:], src_ap[i])

        def pcast():
            i = state["cs"]
            if i >= state["ld"] or i >= DC:
                return
            state["cs"] += 1
            bi = i % 2
            s_in = dm.get("cwld", bi)
            need = s_in.n

            @VE
            def _(v, bi=bi, s_in=s_in, need=need):
                v.wait_ge(s_in.sem, need)
                c = dm.get("cwst", bi)
                if c.n:
                    v.wait_ge(c.sem, c.n)
                vC.inc(v.tensor_copy(cast_stage_b[bi][:], cast_stage_f[bi][:]))
                cast_busy[bi] = vC.n
            v_cast = vC.n

            @GE
            def _(g, i=i, bi=bi, v_cast=v_cast):
                g.wait_ge(vC.sem, v_cast)
                dm.put(g, "cwst", bi, dst_ap[i], cast_stage_b[bi][:])
        return {"load": pload, "cast": pcast}

    s1g = nc.sbuf_tensor("xn2_fm", [P, DC * T], BF16, side="right")
    s1 = s1g.__enter__()

    p1 = ExitStack()
    xs = [p1.enter_context(nc.sbuf_tensor(f"xs{i}", [P, D], F32)) for i in range(4)]
    xn = [p1.enter_context(nc.sbuf_tensor(f"xn{i}", [P, D], F32)) for i in range(2)]
    xn2tm = [p1.enter_context(nc.sbuf_tensor(f"xn2tm{i}", [P, D], BF16))
             for i in range(2)]
    xq = p1.enter_context(nc.sbuf_tensor("xq", [P, D], U8))
    xdq = p1.enter_context(nc.sbuf_tensor("xdq", [P, D], F32))
    xhtm = p1.enter_context(nc.sbuf_tensor("xhtm", [P, D], BF16))
    xltm = p1.enter_context(nc.sbuf_tensor("xltm", [P, D], BF16))
    xhst = [p1.enter_context(nc.sbuf_tensor(f"xhst{i}", [P, D], BF16))
            for i in range(2)]
    xlst = [p1.enter_context(nc.sbuf_tensor(f"xlst{i}", [P, D], BF16))
            for i in range(2)]
    nbn = (D + 511) // 512
    bn6 = p1.enter_context(nc.sbuf_tensor("bn6", [P, 6 * nbn], F32))
    mvall = p1.enter_context(nc.sbuf_tensor("mvall", [P, 2 * TK], F32))
    uual = p1.enter_context(nc.sbuf_tensor("uual", [P, 2 * TK], F32))
    rral = p1.enter_context(nc.sbuf_tensor("rral", [P, 2 * TK], F32))
    nt1 = p1.enter_context(nc.sbuf_tensor("nt1", [P, 2 * TK], F32))
    nt2 = p1.enter_context(nc.sbuf_tensor("nt2", [P, 2 * TK], F32))
    zint = p1.enter_context(nc.sbuf_tensor("zint", [P, 2 * TK], mybir.dt.int32))
    c1al = p1.enter_context(nc.sbuf_tensor("c1al", [P, TK], F32))
    nb1al = p1.enter_context(nc.sbuf_tensor("nb1al", [P, TK], F32))
    nb2al = p1.enter_context(nc.sbuf_tensor("nb2al", [P, TK], F32))
    am = p1.enter_context(nc.sbuf_tensor("am", [P, DC], F32))
    sc_q = p1.enter_context(nc.sbuf_tensor("sc_q", [P, DC], F32))
    sc_iv = p1.enter_context(nc.sbuf_tensor("sc_iv", [P, DC], F32))
    cast_stage_f = [p1.enter_context(nc.sbuf_tensor(f"cstf{i}", [P, DC * P], F32))
                    for i in range(2)]
    cast_stage_b = [p1.enter_context(nc.sbuf_tensor(f"cstb{i}", [P, DC * P], BF16))
                    for i in range(2)]

    pump_wq = make_cast_job(w_r["q"], wbf_sc["q"], "cwq")

    xsa_free = [0, 0]
    for tc in range(TK):
        bi = tc % 2

        @SY
        def _(sy, tc=tc, bi=bi):
            if xsa_free[bi]:
                sy.wait_ge(vC.sem, xsa_free[bi])
            dm.put(sy, "p1ax", bi, xs[bi][:], x_d[tc * P:(tc + 1) * P, :])
        s_x = dm.get("p1ax", bi)
        need_x = s_x.n

        @VE
        def _(v, tc=tc, bi=bi, s_x=s_x, need_x=need_x):
            v.wait_ge(s_x.sem, need_x)
            v.wait_ge(vC.sem, vC.n)
            for b in range(nbn):
                w = min(512, D - b * 512)
                mm = v.bn_stats(bn6[:, 6 * b:6 * b + 6],
                                xs[bi][:, b * 512:b * 512 + w])
            vC.inc(mm)
            v.wait_ge(vC.sem, vC.n)
            vC.inc(v.bn_aggr(mvall[:, 2 * tc:2 * tc + 2], bn6[:, :]))
            xsa_free[bi] = vC.n

    mean_v = mvall[:].rearrange("p (t k) -> p t k", k=2)[:, :, 0:1]
    var_v = mvall[:].rearrange("p (t k) -> p t k", k=2)[:, :, 1:2]

    @VE
    def _(v):
        v.wait_ge(vC.sem, vC.n)
        u3 = uual[:, 0:TK].rearrange("p (t k) -> p t k", k=1)
        vC.inc(v.tensor_scalar(u3, var_v, EPS_LN, None, OP.add))
        v.wait_ge(vC.sem, vC.n)
        vC.inc(v.reciprocal(nt1[:, 0:TK], uual[:, 0:TK]))
        v.wait_ge(vC.sem, vC.n)
        vC.inc(v.tensor_tensor(nt2[:, 0:TK].rearrange("p (t k) -> p t k", k=1),
                               var_v,
                               nt1[:, 0:TK].rearrange("p (t k) -> p t k", k=1),
                               OP.mult))
        v.wait_ge(vC.sem, vC.n)
        vC.inc(v.tensor_tensor(uual[:, TK:2 * TK], nt2[:, 0:TK],
                               epst[:, 0:1].broadcast_to([P, TK]), OP.add))
        v.wait_ge(vC.sem, vC.n)
        vC.inc(v.tensor_copy(zint[:, :], uual[:].bitcast(mybir.dt.int32)))
        v.wait_ge(vC.sem, vC.n)
        vC.inc(v.tensor_scalar(zint[:, :], zint[:, :], 1, None,
                               OP.logical_shift_right))
        v.wait_ge(vC.sem, vC.n)
        vC.inc(v.tensor_scalar(zint[:, :], zint[:, :], -1, 0x5F3759DF,
                               OP.mult, OP.add))
        v.wait_ge(vC.sem, vC.n)
        vC.inc(v.tensor_copy(rral[:, :], zint[:].bitcast(F32)))
        for _ in range(5):
            v.wait_ge(vC.sem, vC.n)
            vC.inc(v.tensor_tensor(nt1[:, :], uual[:, :], rral[:, :], OP.mult))
            v.wait_ge(vC.sem, vC.n)
            vC.inc(v.tensor_tensor(nt2[:, :], nt1[:, :], rral[:, :], OP.mult))
            v.wait_ge(vC.sem, vC.n)
            vC.inc(v.tensor_scalar(nt2[:, :], nt2[:, :], -0.5, 1.5,
                                   OP.mult, OP.add))
            v.wait_ge(vC.sem, vC.n)
            vC.inc(v.tensor_tensor(rral[:, :], rral[:, :], nt2[:, :], OP.mult))
        v.wait_ge(vC.sem, vC.n)
        vC.inc(v.tensor_tensor(c1al[:, :], rral[:, 0:TK], rral[:, TK:2 * TK],
                               OP.mult))
        v.wait_ge(vC.sem, vC.n)
        v.tensor_tensor(nt1[:, 0:TK].rearrange("p (t k) -> p t k", k=1),
                        mean_v, rral[:, 0:TK].rearrange("p (t k) -> p t k", k=1),
                        OP.mult)
        vC.inc(v.tensor_tensor(
            nt2[:, 0:TK].rearrange("p (t k) -> p t k", k=1), mean_v,
            c1al[:].rearrange("p (t k) -> p t k", k=1), OP.mult))
        v.wait_ge(vC.sem, vC.n)
        v.tensor_scalar(nb1al[:, :], nt1[:, 0:TK], -1.0, None, OP.mult)
        vC.inc(v.tensor_scalar(nb2al[:, :], nt2[:, 0:TK], -1.0, None, OP.mult))
    v_scales = vC.n

    xn_free_v = [0, 0]
    xn2_used_t = [0, 0]
    bank_free = {}

    xs_free = [0, 0, 0, 0]
    pend_spill = [None]

    def emit_spill():
        if pend_spill[0] is None:
            return
        stc, sbi, sv = pend_spill[0]
        pend_spill[0] = None

        @SE
        def _(s, stc=stc, sbi=sbi, sv=sv):
            s.wait_ge(vC.sem, sv)
            dm.put(s, "p1st", sbi, xh_sc[stc], xhst[sbi][:])
            dm.put(s, "p1st", sbi, xl_sc[stc], xlst[sbi][:])

    p1a_guard = list(xsa_free)
    for tc in range(TK):
        bi = tc % 2
        bx = tc % 4
        pump_wq["load"]()

        @SY
        def _(sy, tc=tc, bx=bx):
            if xs_free[bx]:
                sy.wait_ge(sC.sem, xs_free[bx])
            elif bx < 2 and p1a_guard[bx]:
                sy.wait_ge(vC.sem, p1a_guard[bx])
            dm.put(sy, "p1x", bx % 2, xs[bx][:], x_d[tc * P:(tc + 1) * P, :])
        s_x = dm.get("p1x", bx % 2)
        need_x = s_x.n

        @SE
        def _(s, tc=tc, bi=bi, bx=bx, s_x=s_x, need_x=need_x):
            s.wait_ge(s_x.sem, need_x)
            s.wait_ge(vC.sem, v_scales)
            if xn_free_v[bi]:
                s.wait_ge(vC.sem, xn_free_v[bi])
            if xn2_used_t[bi]:
                s.wait_ge(tC.sem, xn2_used_t[bi])
            s.activation(xn[bi][:, :], xs[bx][:, :], ACT.Identity,
                         bias=nb1al[:, tc:tc + 1], scale=rral[:, tc:tc + 1])
            sC.inc(s.activation(xn2tm[bi][:, :], xs[bx][:, :], ACT.Identity,
                                bias=nb2al[:, tc:tc + 1],
                                scale=c1al[:, tc:tc + 1]))
            xs_free[bx] = sC.n
        s_xn = sC.n
        emit_spill()

        n_grp = (DC + 3) // 4

        for grp in range(n_grp):
            dc0 = grp * 4
            ndc = min(4, DC - dc0)
            bank = 4 + grp % 4

            @TE
            def _(t, dc0=dc0, ndc=ndc, bank=bank, bi=bi, s_xn=s_xn,
                  last=(grp == n_grp - 1)):
                s_const.wait(t)
                t.wait_ge(sC.sem, s_xn)
                fv = bank_free.get(bank, 0)
                if fv:
                    t.wait_ge(sC.sem if fv[0] == "s" else vC.sem, fv[1])
                for k in range(ndc):
                    mm = t.matmul(ps[bank][:, k * P:(k + 1) * P],
                                  xn2tm[bi][:, (dc0 + k) * P:(dc0 + k + 1) * P],
                                  ident[:, :], start=True, stop=True)
                n = tC.inc(mm)
                if last:
                    xn2_used_t[bi] = n
            t_tr = tC.n

            @VE
            def _(v, dc0=dc0, ndc=ndc, bank=bank, tc=tc, t_tr=t_tr):
                v.wait_ge(tC.sem, t_tr)
                dst = s1[:].rearrange("p (c t) -> p c t", c=DC)[
                    :, dc0:dc0 + ndc, tc * P:(tc + 1) * P]
                src_ap = ps[bank][:].rearrange("p (c k) -> p c k", c=4)[
                    :, 0:ndc, :]
                vC.inc(v.tensor_copy(dst, src_ap))
                bank_free[bank] = ("v", vC.n)

        @VE
        def _(v, tc=tc, bi=bi, s_xn=s_xn):
            v.wait_ge(sC.sem, s_xn)
            v.wait_ge(vC.sem, v_const)
            xn3 = xn[bi][:].rearrange("p (c k) -> p c k", c=DC)
            vC.inc(v.tensor_reduce(am[:, :], xn3, AX, OP.max,
                                   apply_absolute_value=True))
            xn_free_v[bi] = vC.n
            v.wait_ge(vC.sem, vC.n)
            vC.inc(v.tensor_scalar(am[:, :], am[:, :], 1.0e-4, None, OP.max))
            v.wait_ge(vC.sem, vC.n)
            vC.inc(v.reciprocal(sc_q[:, :], am[:, :]))
            v.wait_ge(vC.sem, vC.n)
            v.tensor_scalar(sc_q[:, :], sc_q[:, :], FP8_MAX, None, OP.mult)
            vC.inc(v.tensor_scalar(sc_iv[:, :], am[:, :], 1.0 / FP8_MAX, None,
                                   OP.mult))
            v.wait_ge(vC.sem, vC.n)
            scb = sc_q[:].unsqueeze(2).broadcast_to([P, DC, P])
            vC.inc(v.tensor_tensor(
                xq[:].bitcast(F8).rearrange("p (c k) -> p c k", c=DC),
                xn3, scb, OP.mult))
            v.wait_ge(vC.sem, vC.n)
            scivb = sc_iv[:].unsqueeze(2).broadcast_to([P, DC, P])
            vC.inc(v.tensor_tensor(
                xdq[:].rearrange("p (c k) -> p c k", c=DC),
                xq[:].bitcast(F8).rearrange("p (c k) -> p c k", c=DC),
                scivb, OP.mult))
            v.wait_ge(vC.sem, vC.n)
            vC.inc(v.tensor_copy(xhtm[:, :], xdq[:, :]))
            v.wait_ge(vC.sem, vC.n)
            vC.inc(v.tensor_tensor(xltm[:, :], xdq[:, :], xhtm[:, :],
                                   OP.subtract))
        v_tm = vC.n

        for si, (src_t, kind) in enumerate([(xhtm, "xh"), (xltm, "xl")]):
            for grp in range(n_grp):
                dc0 = grp * 4
                ndc = min(4, DC - dc0)
                bank = (si * n_grp + grp) % 4

                @TE
                def _(t, src_t=src_t, dc0=dc0, ndc=ndc, bank=bank, v_tm=v_tm):
                    t.wait_ge(vC.sem, v_tm)
                    fv = bank_free.get(bank, 0)
                    if fv:
                        t.wait_ge(sC.sem if fv[0] == "s" else vC.sem, fv[1])
                    for k in range(ndc):
                        mm = t.matmul(ps[bank][:, k * P:(k + 1) * P],
                                      src_t[:, (dc0 + k) * P:(dc0 + k + 1) * P],
                                      ident[:, :], start=True, stop=True)
                    tC.inc(mm)
                t_tr = tC.n

                @VE
                def _(v, kind=kind, dc0=dc0, ndc=ndc, bank=bank, bi=bi,
                      t_tr=t_tr):
                    v.wait_ge(tC.sem, t_tr)
                    sst = dm.get("p1st", bi)
                    sst.wait(v)
                    dst = (xhst if kind == "xh" else xlst)[bi][
                        :, dc0 * P:(dc0 + ndc) * P]
                    vC.inc(v.tensor_copy(dst, ps[bank][:, 0:ndc * P]))
                    bank_free[bank] = ("v", vC.n)

        pend_spill[0] = (tc, bi, vC.n)
        pump_wq["cast"]()

    emit_spill()
    for _i in range(DC):
        pump_wq["load"]()
        pump_wq["cast"]()

    if debug:
        @SY
        def _(sy):
            for c in (vC, sC, tC):
                sy.wait_ge(c.sem, c.n)
            dm.put(sy, "dbg", 0, xn2_dbg[:, :], s1[:])

    join(SY, VE, SE, TE, GE)
    p1.close()

    p2a = ExitStack()
    kfm = p2a.enter_context(nc.sbuf_tensor("kfm", [P, DC * T], BF16))
    p2 = ExitStack()
    wbuf = [p2.enter_context(nc.sbuf_tensor(f"wbuf{i}", [P, DC * P], BF16))
            for i in range(3)]
    pstage = [p2.enter_context(nc.sbuf_tensor(f"pstage{i}", [P, TW], BF16))
              for i in range(2)]
    cast_stage_f = [p2.enter_context(nc.sbuf_tensor(f"c2f{i}", [P, DC * P], F32))
                    for i in range(2)]
    cast_stage_b = [p2.enter_context(nc.sbuf_tensor(f"c2b{i}", [P, DC * P], BF16))
                    for i in range(2)]
    wfband = [p2.enter_context(nc.sbuf_tensor(f"wfb{i}", [P, D], F32))
              for i in range(2)]
    wdq = p2.enter_context(nc.sbuf_tensor("wdq", [P, D], F32))
    wq8 = p2.enter_context(nc.sbuf_tensor("wq8", [P, D], U8))
    whb = p2.enter_context(nc.sbuf_tensor("whb", [P, D], BF16))
    wlb = p2.enter_context(nc.sbuf_tensor("wlb", [P, D], BF16))
    NB = DC * DC
    wam_all = p2.enter_context(nc.sbuf_tensor("wam_all", [P, NB], F32))
    wtmp = p2.enter_context(nc.sbuf_tensor("wtmp", [P, NB], F32))
    wsc_all = p2.enter_context(nc.sbuf_tensor("wsc_all", [P, NB], F32))
    wsiv_all = p2.enter_context(nc.sbuf_tensor("wsiv_all", [P, NB], F32))

    wfa = {"dc": 0, "free": [0, 0]}

    def pump_wf_amax():
        dc = wfa["dc"]
        if dc >= DC:
            return
        wfa["dc"] += 1
        bi = dc % 2

        @SY
        def _(sy, dc=dc, bi=bi):
            if wfa["free"][bi]:
                sy.wait_ge(vC.sem, wfa["free"][bi])
            dm.put(sy, "wfa", bi, wfband[bi][:], wf_t[dc * P:(dc + 1) * P, :])
        s_wf = dm.get("wfa", bi)
        need = s_wf.n

        @VE
        def _(v, dc=dc, bi=bi, s_wf=s_wf, need=need):
            v.wait_ge(s_wf.sem, need)
            v.wait_ge(vC.sem, vC.n)
            w3 = wfband[bi][:].rearrange("p (c k) -> p c k", c=DC)
            vC.inc(v.tensor_reduce(wam_all[:, dc * DC:(dc + 1) * DC], w3, AX,
                                   OP.max, apply_absolute_value=True))
            wfa["free"][bi] = vC.n

    def emit_wf_tree():
        sz = P // 2
        step = 0
        prev_v = vC.n
        while sz >= 1:
            @GE
            def _(sy, sz=sz, prev_v=prev_v, step=step):
                sy.wait_ge(vC.sem, prev_v)
                dm.put(sy, "tree", step % 2, wtmp[0:sz, :], wam_all[sz:2 * sz, :])
            s_t = dm.get("tree", step % 2)
            need_t = s_t.n

            @VE
            def _(v, sz=sz, s_t=s_t, need_t=need_t):
                v.wait_ge(s_t.sem, need_t)
                v.wait_ge(vC.sem, vC.n)
                vC.inc(v.tensor_tensor(wam_all[0:sz, :], wam_all[0:sz, :],
                                       wtmp[0:sz, :], OP.max))
            prev_v = vC.n
            sz //= 2
            step += 1
        k = 1
        step = 0
        while k < P:
            @GE
            def _(sy, k=k, prev_v=prev_v, step=step):
                if step == 0:
                    sy.wait_ge(vC.sem, prev_v)
                else:
                    c = dm.get("bcst", (step - 1) % 2)
                    sy.wait_ge(c.sem, c.n)
                dm.put(sy, "bcst", step % 2, wam_all[k:min(2 * k, P), :],
                       wam_all[0:min(k, P - k), :])
            k *= 2
            step += 1
        s_bc = dm.get("bcst", (step - 1) % 2)
        need_bc = s_bc.n

        @VE
        def _(v, s_bc=s_bc, need_bc=need_bc):
            v.wait_ge(s_bc.sem, need_bc)
            v.wait_ge(vC.sem, v_const)
            vC.inc(v.tensor_scalar(wam_all[:, :], wam_all[:, :], 1.0e-4, None,
                                   OP.max))
            v.wait_ge(vC.sem, vC.n)
            vC.inc(v.reciprocal(wsc_all[:, :], wam_all[:, :]))
            v.wait_ge(vC.sem, vC.n)
            v.tensor_scalar(wsc_all[:, :], wsc_all[:, :], FP8_MAX, None, OP.mult)
            vC.inc(v.tensor_scalar(wsiv_all[:, :], wam_all[:, :], 1.0 / FP8_MAX,
                                   None, OP.mult))

    wfq = {"dc": 0, "free": [0, 0], "st": 0}

    def pump_wf_quant():
        dc = wfq["dc"]
        if dc >= DC:
            return
        wfq["dc"] += 1
        bi = dc % 2

        @SY
        def _(sy, dc=dc, bi=bi):
            if wfq["free"][bi]:
                sy.wait_ge(vC.sem, wfq["free"][bi])
            dm.put(sy, "wfq", bi, wfband[bi][:], wf_t[dc * P:(dc + 1) * P, :])
        s_wf = dm.get("wfq", bi)
        need = s_wf.n

        @VE
        def _(v, dc=dc, bi=bi, s_wf=s_wf, need=need):
            v.wait_ge(s_wf.sem, need)
            v.wait_ge(vC.sem, vC.n)
            c = dm.get("whlst", 0)
            if c.n:
                v.wait_ge(c.sem, c.n)
            w3 = wfband[bi][:].rearrange("p (c k) -> p c k", c=DC)
            wscb = wsc_all[:, dc * DC:(dc + 1) * DC].unsqueeze(2) \
                .broadcast_to([P, DC, P])
            vC.inc(v.tensor_tensor(
                wq8[:].bitcast(F8).rearrange("p (c k) -> p c k", c=DC),
                w3, wscb, OP.mult))
            wfq["free"][bi] = vC.n
            v.wait_ge(vC.sem, vC.n)
            wsivb = wsiv_all[:, dc * DC:(dc + 1) * DC].unsqueeze(2) \
                .broadcast_to([P, DC, P])
            vC.inc(v.tensor_tensor(
                wdq[:].rearrange("p (c k) -> p c k", c=DC),
                wq8[:].bitcast(F8).rearrange("p (c k) -> p c k", c=DC),
                wsivb, OP.mult))
            v.wait_ge(vC.sem, vC.n)
            vC.inc(v.tensor_copy(whb[:, :], wdq[:, :]))
            v.wait_ge(vC.sem, vC.n)
            vC.inc(v.tensor_tensor(wlb[:, :], wdq[:, :], whb[:, :], OP.subtract))
        v_whl = vC.n

        @SE
        def _(s, dc=dc, v_whl=v_whl):
            s.wait_ge(vC.sem, v_whl)
            dm.put(s, "whlst", 0, wh_sc[dc], whb[:])
            dm.put(s, "whlst", 0, wl_sc[dc], wlb[:])

    pump_casts = {
        "q": make_cast_job(w_r["k"], wbf_sc["k"], "cwk"),
        "k": make_cast_job(w_r["g"], wbf_sc["g"], "cwg"),
        "g": make_cast_job(w_r["v"], wbf_sc["v"], "cwv"),
        "v": make_cast_job(
            wp_t.rearrange("(c p) d -> c p d", p=P), wpbf_sc, "cwp"),
    }

    wb_free = [0, 0, 0]
    grp_free = {}
    w_cfg = [("q", ACT.Silu, q_sc), ("k", ACT.Silu, None),
             ("g", ACT.Sigmoid, g_sc), ("v", ACT.Silu, v_sc)]
    for wi, (wname, actf, dstsc) in enumerate(w_cfg):
        for oc in range(DC):
            bi = (wi * DC + oc) % 3
            pump_casts[wname]["cast"]()
            pump_casts[wname]["load"]()
            if wname == "q":
                pump_wf_amax()
            elif wname in ("k", "g") and (oc % 2 == 0):
                pump_wf_quant()

            @SY
            def _(sy, wname=wname, oc=oc, bi=bi):
                if wb_free[bi]:
                    sy.wait_ge(tC.sem, wb_free[bi])
                if oc == 0:
                    for par in (0, 1):
                        dm.get("cwst", par).wait(sy)
                dm.put(sy, "p2w", bi, wbuf[bi][:], wbf_sc[wname][oc])
            s_w = dm.get("p2w", bi)
            need_w = s_w.n

            banks = tuple((oc % 2) * 4 + t for t in range(NTT))

            @TE
            def _(t, oc=oc, bi=bi, banks=banks, s_w=s_w, need_w=need_w):
                t.wait_ge(s_w.sem, need_w)
                fs = grp_free.get(banks, 0)
                if fs:
                    t.wait_ge(sC.sem, fs)
                for dc in range(DC):
                    for tt in range(NTT):
                        mm = t.matmul(
                            ps[banks[tt]][:, 0:TW],
                            wbuf[bi][:, dc * P:(dc + 1) * P],
                            s1[:, dc * T + tt * TW: dc * T + (tt + 1) * TW],
                            start=(dc == 0), stop=(dc == DC - 1))
                wb_free[bi] = tC.inc(mm)
            t_proj = tC.n

            if wname == "k":
                @SE
                def _(s, oc=oc, banks=banks, t_proj=t_proj):
                    s.wait_ge(tC.sem, t_proj)
                    for tt in range(NTT):
                        mm = s.activation(
                            kfm[:, oc * T + tt * TW: oc * T + (tt + 1) * TW],
                            ps[banks[tt]][:, 0:TW], ACT.Silu)
                    grp_free[banks] = sC.inc(mm)
            else:
                for tt in range(NTT):
                    sbi = (oc * NTT + tt) % 2

                    @SE
                    def _(s, oc=oc, tt=tt, sbi=sbi, banks=banks, actf=actf,
                          dstsc=dstsc, t_proj=t_proj):
                        if tt == 0:
                            s.wait_ge(tC.sem, t_proj)
                        sst = dm.get("p2st", sbi)
                        sst.wait(s)
                        sC.inc(s.activation(pstage[sbi][:, :],
                                            ps[banks[tt]][:, 0:TW], actf))
                        s.wait_ge(sC.sem, sC.n)
                        dm.put(s, "p2st", sbi, dstsc[tt, oc], pstage[sbi][:])
                grp_free[banks] = sC.n
        pump_casts[wname]["cast"]()
        if wname == "q":
            emit_wf_tree()
            wfq["free"] = list(wfa["free"])

    join(SY, VE, SE, TE, GE)
    s1g.__exit__(None, None, None)
    p2.close()

    p3 = ExitStack()
    vtm = p3.enter_context(nc.sbuf_tensor("vtm", [P, TK * D], BF16))
    qsl = [p3.enter_context(nc.sbuf_tensor(f"qsl{i}", [P, DC * TW], BF16,
                                           side="right")) for i in range(2)]
    stsb = p3.enter_context(nc.sbuf_tensor("stsb", [P, TK * TW], BF16,
                                           side="right"))
    ostage = [p3.enter_context(nc.sbuf_tensor(f"ostage{i}", [P, TW], BF16,
                                              side="right")) for i in range(2)]
    gbuf = [p3.enter_context(nc.sbuf_tensor(f"gbuf{i}", [P, TW], BF16,
                                            side="right")) for i in range(4)]
    vbst = [p3.enter_context(nc.sbuf_tensor(f"vbst{i}", [P, DC * P], BF16,
                                            side="right")) for i in range(2)]

    g_free = [0, 0, 0, 0]
    sbank_free = {}
    obank_free = {}
    vb_free = [0, 0]
    qsl_mm = [0, 0]
    for it in range(NTT):
        jmax = it * IT + IT - 1
        qpar = it % 2

        @SY
        def _(sy, it=it, qpar=qpar):
            if qsl_mm[qpar]:
                sy.wait_ge(tC.sem, qsl_mm[qpar])
            dm.put(sy, "p3q", qpar,
                   qsl[qpar][:].rearrange("p (c w) -> p c w", c=DC),
                   q_sc[it].rearrange("c p w -> p c w"))
        s_q = dm.get("p3q", qpar)
        need_q = s_q.n

        for J in range(jmax + 1):
            off = max(0, (J - it * IT)) * P
            bank = J % 2

            @TE
            def _(t, J=J, off=off, bank=bank, s_q=s_q, need_q=need_q,
                  qpar=qpar, jmax=jmax):
                if J == 0:
                    t.wait_ge(s_q.sem, need_q)
                fv = sbank_free.get(("s", bank), 0)
                if fv:
                    t.wait_ge(vC.sem, fv)
                for oc in range(DC):
                    mm = t.matmul(
                        ps[bank][:, off:TW],
                        kfm[:, oc * T + J * P: oc * T + (J + 1) * P],
                        qsl[qpar][:, oc * TW + off: (oc + 1) * TW],
                        start=(oc == 0), stop=(oc == DC - 1))
                n = tC.inc(mm)
                if J == jmax:
                    qsl_mm[qpar] = n
            t_s = tC.n

            @VE
            def _(v, it=it, J=J, off=off, bank=bank, t_s=t_s):
                v.wait_ge(tC.sem, t_s)
                s_const.wait(v)
                diag = J >= it * IT
                if diag:
                    mm = v.tensor_tensor(stsb[:, J * TW + off: J * TW + off + P],
                                         ps[bank][:, off:off + P], trimask[:, :],
                                         OP.mult)
                    if TW - off - P > 0:
                        mm = v.tensor_copy(stsb[:, J * TW + off + P: (J + 1) * TW],
                                           ps[bank][:, off + P:TW])
                else:
                    mm = v.tensor_copy(stsb[:, J * TW: (J + 1) * TW],
                                       ps[bank][:, 0:TW])
                vC.inc(mm)
                sbank_free[("s", bank)] = vC.n
        st_all = vC.n

        for jb in range(IT):
            J = it * IT + jb
            vbi = jb % 2

            @GE
            def _(sy, it=it, jb=jb, vbi=vbi):
                if vb_free[vbi]:
                    sy.wait_ge(tC.sem, vb_free[vbi])
                dm.put(sy, "p3vb", vbi,
                       vbst[vbi][:].rearrange("p (c w) -> p c w", c=DC),
                       v_sc[it].rearrange("c p w -> p c w")[
                           :, :, jb * P:(jb + 1) * P])
            s_vb = dm.get("p3vb", vbi)
            need_vb = s_vb.n

            n_vgrp = (DC + 3) // 4
            for grp in range(n_vgrp):
                bank = 6 + grp % 2
                nk = min(4, DC - grp * 4)

                @TE
                def _(t, vbi=vbi, grp=grp, bank=bank, nk=nk, s_vb=s_vb,
                      need_vb=need_vb, last=(grp == n_vgrp - 1)):
                    t.wait_ge(s_vb.sem, need_vb)
                    s_const.wait(t)
                    fs = sbank_free.get(bank, 0)
                    if fs:
                        t.wait_ge(vC.sem, fs)
                    for k in range(nk):
                        oc = grp * 4 + k
                        mm = t.matmul(ps[bank][:, k * P:(k + 1) * P],
                                      vbst[vbi][:, oc * P:(oc + 1) * P],
                                      ident[:, :], start=True, stop=True)
                    n = tC.inc(mm)
                    if last:
                        vb_free[vbi] = n
                t_vtr = tC.n

                @VE
                def _(v, J=J, grp=grp, bank=bank, nk=nk, t_vtr=t_vtr):
                    v.wait_ge(tC.sem, t_vtr)
                    vC.inc(v.tensor_copy(
                        vtm[:, J * D + grp * 4 * P: J * D + (grp * 4 + nk) * P],
                        ps[bank][:, 0:nk * P]))
                    sbank_free[bank] = vC.n
        vtm_ready = vC.n

        for ec in range(DC):
            bank = 2 + ec % 3
            gbi = ec % 4
            obi = ec % 2

            @GE
            def _(sy, it=it, ec=ec, gbi=gbi):
                if g_free[gbi]:
                    sy.wait_ge(vC.sem, g_free[gbi])
                dm.put(sy, "p3g", gbi, gbuf[gbi][:], g_sc[it, ec])
            s_g = dm.get("p3g", gbi)
            need_g = s_g.n

            @TE
            def _(t, it=it, ec=ec, bank=bank, jmax=jmax, st_all=st_all,
                  vtm_ready=vtm_ready):
                t.wait_ge(vC.sem, st_all)
                if ec == 0:
                    t.wait_ge(vC.sem, vtm_ready)
                fv = obank_free.get(bank, 0)
                if fv:
                    t.wait_ge(vC.sem, fv)
                for J in range(jmax + 1):
                    off = max(0, (J - it * IT)) * P
                    mm = t.matmul(
                        ps[bank][:, off:TW],
                        vtm[:, J * D + ec * P: J * D + (ec + 1) * P],
                        stsb[:, J * TW + off: (J + 1) * TW],
                        start=(J == 0), stop=(J == jmax))
                tC.inc(mm)
            t_sv = tC.n

            @VE
            def _(v, bank=bank, gbi=gbi, obi=obi, t_sv=t_sv, s_g=s_g,
                  need_g=need_g):
                v.wait_ge(tC.sem, t_sv)
                v.wait_ge(s_g.sem, need_g)
                sost = dm.get("p3o", obi)
                sost.wait(v)
                n = vC.inc(v.tensor_tensor(ostage[obi][:, :], ps[bank][:, 0:TW],
                                           gbuf[gbi][:, :], OP.mult))
                obank_free[bank] = n
                g_free[gbi] = n
            v_o = vC.n

            @SE
            def _(s, it=it, ec=ec, obi=obi, v_o=v_o):
                s.wait_ge(vC.sem, v_o)
                dm.put(s, "p3o", obi, o_sc[it, ec], ostage[obi][:])

    s3_done = qsl_mm[(NTT - 1) % 2]

    @SY
    def _(sy, s3_done=s3_done):
        sy.wait_ge(tC.sem, s3_done)
        dm.get("whlst", 0).wait(sy)
        dm.put(sy, "whpre", 0,
               kfm[:].rearrange("p (c w) -> p c w", c=DC),
               wh_sc.rearrange("c p w -> p c w"))

    join(SY, VE, SE, TE, GE)
    p3.close()

    pf = ExitStack()
    whs = kfm
    wls = pf.enter_context(nc.sbuf_tensor("wls", [P, DC * D], BF16))
    xhf = [pf.enter_context(nc.sbuf_tensor(f"xhf{i}", [P, D], BF16))
           for i in range(2)]
    xlf = [pf.enter_context(nc.sbuf_tensor(f"xlf{i}", [P, D], BF16))
           for i in range(2)]
    fst = [pf.enter_context(nc.sbuf_tensor(f"fst{i}", [P, D], U16))
           for i in range(2)]

    @SE
    def _(s):
        dm.put(s, "whl_in", 0,
               wls[:].rearrange("p (c w) -> p c w", c=DC),
               wl_sc.rearrange("c p w -> p c w"))
    s_whl = dm.get("whl_in", 0)
    need_whl = s_whl.n

    xf_free = [0, 0]
    fbank_free = {}
    for tc in range(TK):
        bi = tc % 2

        @SY
        def _(sy, tc=tc, bi=bi):
            if xf_free[bi]:
                sy.wait_ge(tC.sem, xf_free[bi])
            dm.put(sy, "pfx", bi, xhf[bi][:], xh_sc[tc])
            dm.put(sy, "pfx", bi, xlf[bi][:], xl_sc[tc])
        s_xf = dm.get("pfx", bi)
        need_xf = s_xf.n

        banks = tuple((tc % 2) * 4 + t for t in range(NOT_))

        @TE
        def _(t, bi=bi, banks=banks, s_xf=s_xf, need_xf=need_xf,
              first=(tc == 0)):
            t.wait_ge(s_xf.sem, need_xf)
            if fbank_free.get(banks, 0):
                t.wait_ge(vC.sem, fbank_free[banks])
            chains = [(xhf[bi], whs), (xlf[bi], whs), (xhf[bi], wls)]
            nch = len(chains)
            for ci, (lhs, rhs) in enumerate(chains):
                if first and ci == nch - 1:
                    t.wait_ge(s_whl.sem, need_whl)
                for dc in range(DC):
                    for ot in range(NOT_):
                        mm = t.matmul(
                            ps[banks[ot]][:, 0:OW],
                            lhs[:, dc * P:(dc + 1) * P],
                            rhs[:, dc * D + ot * OW: dc * D + (ot + 1) * OW],
                            start=(ci == 0 and dc == 0),
                            stop=(ci == nch - 1 and dc == DC - 1))
            xf_free[bi] = tC.inc(mm)
        t_ffn = tC.n

        @VE
        def _(v, bi=bi, banks=banks, t_ffn=t_ffn):
            v.wait_ge(tC.sem, t_ffn)
            sfst = dm.get("pfst", bi)
            sfst.wait(v)
            for ot in range(NOT_):
                mm = v.tensor_copy(
                    fst[bi][:].bitcast(BF16)[:, ot * OW:(ot + 1) * OW],
                    ps[banks[ot]][:, 0:OW])
            fbank_free[banks] = vC.inc(mm)
        v_ffn = vC.n

        @SE
        def _(s, tc=tc, bi=bi, v_ffn=v_ffn):
            s.wait_ge(vC.sem, v_ffn)
            dm.put(s, "pfst", bi, ffn_sc[tc], fst[bi][:])

    join(SY, VE, SE, TE, GE)
    pf.close()
    p2a.close()

    p4 = ExitStack()
    wps = p4.enter_context(nc.sbuf_tensor("wps", [P, DC * D], BF16))
    osl = [p4.enter_context(nc.sbuf_tensor(f"osl{i}", [P, DC * TW], BF16))
           for i in range(2)]
    xrow = [p4.enter_context(nc.sbuf_tensor(f"xrow{i}", [P, D], F32))
            for i in range(2)]
    frow = [p4.enter_context(nc.sbuf_tensor(f"frow{i}", [P, D], U16))
            for i in range(2)]
    tmpv = p4.enter_context(nc.sbuf_tensor("tmpv", [P, D], F32))
    orow = [p4.enter_context(nc.sbuf_tensor(f"orow{i}", [P, D], F32))
            for i in range(2)]

    @SY
    def _(sy):
        dm.put(sy, "p4wp", 0,
               wps[:].rearrange("p (c w) -> p c w", c=DC),
               wpbf_sc.rearrange("c p w -> p c w"))
    s_wp = dm.get("p4wp", 0)
    need_wp = s_wp.n

    row_free = [0, 0]
    pbank_free = {}
    osl_mm = [0, 0]
    for it in range(NTT):
        opar = it % 2

        @SY
        def _(sy, it=it, opar=opar):
            if osl_mm[opar]:
                sy.wait_ge(tC.sem, osl_mm[opar])
            dm.put(sy, "p4o", opar,
                   osl[opar][:].rearrange("p (c w) -> p c w", c=DC),
                   o_sc[it].rearrange("c p w -> p c w"))
        s_o = dm.get("p4o", opar)
        need_o = s_o.n

        for icl in range(IT):
            ic = it * IT + icl
            bi = ic % 2
            banks = tuple((ic % 2) * 4 + t for t in range(NOT_))

            @SY
            def _(sy, ic=ic, bi=bi):
                if row_free[bi]:
                    sy.wait_ge(vC.sem, row_free[bi])
                dm.put(sy, "p4r", bi, xrow[bi][:], x_d[ic * P:(ic + 1) * P, :])
                dm.put(sy, "p4r", bi, frow[bi][:], ffn_sc[ic])
            s_r = dm.get("p4r", bi)
            need_rows = s_r.n

            @TE
            def _(t, icl=icl, banks=banks, first=(it == 0 and icl == 0),
                  s_o=s_o, need_o=need_o, opar=opar, last=(icl == IT - 1)):
                if first:
                    t.wait_ge(s_wp.sem, need_wp)
                t.wait_ge(s_o.sem, need_o)
                if pbank_free.get(banks, 0):
                    t.wait_ge(vC.sem, pbank_free[banks])
                for ec in range(DC):
                    for ot in range(NOT_):
                        mm = t.matmul(
                            ps[banks[ot]][:, 0:OW],
                            osl[opar][:, ec * TW + icl * P: ec * TW + (icl + 1) * P],
                            wps[:, ec * D + ot * OW: ec * D + (ot + 1) * OW],
                            start=(ec == 0), stop=(ec == DC - 1))
                n = tC.inc(mm)
                if last:
                    osl_mm[opar] = n
            t_p4 = tC.n

            @VE
            def _(v, bi=bi, banks=banks, t_p4=t_p4, s_r=s_r,
                  need_rows=need_rows):
                v.wait_ge(tC.sem, t_p4)
                v.wait_ge(s_r.sem, need_rows)
                v.wait_ge(vC.sem, vC.n)
                sorow = dm.get("p4out", bi)
                sorow.wait(v)
                for ot in range(NOT_):
                    sl = slice(ot * OW, (ot + 1) * OW)
                    mm = v.scalar_tensor_tensor(tmpv[:, sl], ps[banks[ot]][:, 0:OW],
                                                1.0, xrow[bi][:, sl],
                                                OP.mult, OP.add)
                vC.inc(mm)
                v.wait_ge(vC.sem, vC.n)
                for ot in range(NOT_):
                    sl = slice(ot * OW, (ot + 1) * OW)
                    mm = v.tensor_tensor(orow[bi][:, sl], tmpv[:, sl],
                                         frow[bi][:].bitcast(BF16)[:, sl], OP.add)
                n = vC.inc(mm)
                row_free[bi] = n
                pbank_free[banks] = n
            v_p4 = vC.n

            @SE
            def _(s, ic=ic, bi=bi, v_p4=v_p4):
                s.wait_ge(vC.sem, v_p4)
                dm.put(s, "p4out", bi, out_d[ic * P:(ic + 1) * P, :], orow[bi][:])

    @SY
    def _(sy):
        for c in dm.all():
            c.wait(sy)

    p4.close()
    es.close()
    return nc


def make_inputs(x_c, Wq, Wk, Wv, Wg, Wp, Wf, DC=16):
    bf = ml_dtypes.bfloat16

    def reblock(W):
        W4 = np.asarray(W, np.float32).reshape(DC, P, DC, P)
        return np.ascontiguousarray(W4.transpose(0, 3, 2, 1)
                                    ).reshape(DC, P, DC * P)

    return {
        "x": np.ascontiguousarray(np.asarray(x_c, np.float32)),
        "wq_r": reblock(Wq), "wk_r": reblock(Wk),
        "wg_r": reblock(Wg), "wv_r": reblock(Wv),
        "wp_t": np.ascontiguousarray(np.asarray(Wp, np.float32).T),
        "wf_t": np.ascontiguousarray(np.asarray(Wf, np.float32).T),
        "ident": np.eye(P, dtype=np.float32).astype(bf),
        "trimask": np.triu(np.ones((P, P), np.float32)).astype(bf),
    }


def kernel(L0_attn_x, L0_attn_Wq, L0_attn_Wk, L0_attn_Wv, L0_attn_Wg,
           L0_attn_Wp, L0_ffn_weight):
    x = np.asarray(L0_attn_x, np.float32)
    B = x.shape[0]
    nc = build_program(TK=16, DC=16, debug=False)
    base = make_inputs(x[0], L0_attn_Wq, L0_attn_Wk, L0_attn_Wv, L0_attn_Wg,
                       L0_attn_Wp, L0_ffn_weight)
    in_maps = []
    for c in range(B):
        m = dict(base)
        m["x"] = np.ascontiguousarray(x[c])
        in_maps.append(m)
    res = run_bass_kernel_spmd(nc, in_maps, core_ids=list(range(B)))
    out = np.stack([np.asarray(r["out"], np.float32) for r in res.results], axis=0)
    return out


# version 21 (final)
# speedup vs baseline: 1.0894x; 1.0894x over previous
import numpy as np
import ml_dtypes
import concourse.bass as bass
import concourse.mybir as mybir
from concourse.bass_utils import run_bass_kernel_spmd
from contextlib import ExitStack

F32 = mybir.dt.float32
BF16 = mybir.dt.bfloat16
U8 = mybir.dt.uint8
U16 = mybir.dt.uint16
F8 = mybir.dt.float8e4
AX = mybir.AxisListType.X
OP = mybir.AluOpType
ACT = mybir.ActivationFunctionType

P = 128
EPS_LN = 1.0e-5
FP8_MAX = 224.0


class Ctr:
    def __init__(self, sem, step=1):
        self.sem = sem
        self.step = step
        self.n = 0

    def inc(self, inst, k=1):
        inst.then_inc(self.sem, self.step * k)
        self.n += self.step * k
        return self.n

    def wait(self, e, n=None):
        n = self.n if n is None else n
        if n > 0:
            e.wait_ge(self.sem, n)


class Streams:
    def __init__(self, nc, es):
        self.nc = nc
        self.es = es
        self.m = {}

    def get(self, name, bi=0):
        key = (name, bi)
        if key not in self.m:
            self.m[key] = Ctr(
                self.es.enter_context(self.nc.semaphore(f"dm_{name}_{bi}")), 16)
        return self.m[key]

    def put(self, e, name, bi, out, in_):
        c = self.get(name, bi)
        c.inc(e.dma_start(out=out, in_=in_))
        return c

    def all(self):
        return list(self.m.values())


def build_program(TK=16, DC=16, debug=False):
    T = TK * P
    D = DC * P
    TW = min(512, T)
    NTT = T // TW
    IT = TW // P
    OW = min(512, D)
    NOT_ = D // OW

    nc = bass.Bass("TRN2", target_bir_lowering=False)

    x_d = nc.dram_tensor("x", [T, D], F32, kind="ExternalInput").ap()
    w_r = {}
    for nm in ("q", "k", "g", "v"):
        w_r[nm] = nc.dram_tensor(f"w{nm}_r", [DC, P, DC * P], F32,
                                 kind="ExternalInput").ap()
    wp_t = nc.dram_tensor("wp_t", [D, D], F32, kind="ExternalInput").ap()
    wf_t = nc.dram_tensor("wf_t", [D, D], F32, kind="ExternalInput").ap()
    ident_d = nc.dram_tensor("ident", [P, P], BF16, kind="ExternalInput").ap()
    trimask_d = nc.dram_tensor("trimask", [P, P], BF16, kind="ExternalInput").ap()
    out_d = nc.dram_tensor("out", [T, D], F32, kind="ExternalOutput").ap()

    sck = "ExternalOutput" if debug else "Internal"
    xh_sc = nc.dram_tensor("xh_sc", [TK, P, D], BF16, kind=sck).ap()
    xl_sc = nc.dram_tensor("xl_sc", [TK, P, D], BF16, kind=sck).ap()
    q_sc = nc.dram_tensor("q_sc", [NTT, DC, P, TW], BF16, kind=sck).ap()
    g_sc = nc.dram_tensor("g_sc", [NTT, DC, P, TW], BF16, kind=sck).ap()
    v_sc = nc.dram_tensor("v_sc", [NTT, DC, P, TW], BF16, kind=sck).ap()
    o_sc = nc.dram_tensor("o_sc", [NTT, DC, P, TW], BF16, kind=sck).ap()
    ffn_sc = nc.dram_tensor("ffn_sc", [TK, P, D], U16, kind=sck).ap()
    wbf_sc = {}
    for nm in ("q", "k", "g", "v"):
        wbf_sc[nm] = nc.dram_tensor(f"wbf_{nm}", [DC, P, DC * P], BF16,
                                    kind="Internal").ap()
    wpbf_sc = nc.dram_tensor("wpbf", [DC, P, D], BF16, kind="Internal").ap()
    wh_sc = nc.dram_tensor("wh_sc", [DC, P, D], BF16, kind="Internal").ap()
    wl_sc = nc.dram_tensor("wl_sc", [DC, P, D], BF16, kind="Internal").ap()
    if debug:
        xn2_dbg = nc.dram_tensor("xn2_dbg", [P, DC * T], BF16,
                                 kind="ExternalOutput").ap()
        kfm_dbg = nc.dram_tensor("kfm_dbg", [P, DC * T], BF16,
                                 kind="ExternalOutput").ap()
        vtm_dbg = nc.dram_tensor("vtm_dbg", [P, TK * D], BF16,
                                 kind="ExternalOutput").ap()

    es = ExitStack()
    ident = nc.alloc_sbuf_tensor("ident_s", [P, P], BF16)
    trimask = nc.alloc_sbuf_tensor("trimask_s", [P, P], BF16)
    c224 = nc.alloc_sbuf_tensor("c224", [P, DC * DC], F32)
    epsk = nc.alloc_sbuf_tensor("epsk", [P, 48], F32)
    epszm = nc.alloc_sbuf_tensor("epszm", [P, 48], F32)
    epst = nc.alloc_sbuf_tensor("epst", [P, 1], F32)
    ps = [nc.alloc_psum_tensor(f"ps{i}", [P, 512], F32) for i in range(8)]

    dm = Streams(nc, es)
    tC = Ctr(es.enter_context(nc.semaphore("tC")))
    vC = Ctr(es.enter_context(nc.semaphore("vC")))
    sC = Ctr(es.enter_context(nc.semaphore("sC")))

    blk = es.enter_context(nc.Block())
    SY, VE, SE, TE, GE = blk.sync, blk.vector, blk.scalar, blk.tensor, blk.gpsimd

    def join(*engines):
        cs = [c for c in ([tC, vC, sC] + dm.all()) if c.n > 0]
        for engf in engines:
            def _f(e, cs=cs):
                for c in cs:
                    e.wait_ge(c.sem, c.n)
            engf(_f)

    @SY
    def _(sy):
        dm.put(sy, "const", 0, ident[:], ident_d[:, :])
        dm.put(sy, "const", 0, trimask[:], trimask_d[:, :])
    s_const = dm.get("const")

    @VE
    def _(v):
        v.memset(c224[:], FP8_MAX)
        vC.inc(v.memset(epsk[:, 0:1], 2.0 ** -16))
        for k in range(1, 48):
            v.wait_ge(vC.sem, vC.n)
            vC.inc(v.tensor_scalar(epsk[:, k:k + 1], epsk[:, k - 1:k], 0.5,
                                   None, OP.mult))
        v.wait_ge(vC.sem, vC.n)
        vC.inc(v.tensor_scalar(epsk[:, :], epsk[:, :], 1.0, -1.0, OP.add, OP.add))
        v.wait_ge(vC.sem, vC.n)
        vC.inc(v.tensor_single_scalar(epszm[:, :], epsk[:, :], 0.0, OP.is_equal))
        v.wait_ge(vC.sem, vC.n)
        vC.inc(v.tensor_tensor(epsk[:, :], epsk[:, :], epszm[:, :], OP.add))
        v.wait_ge(vC.sem, vC.n)
        vC.inc(v.tensor_reduce(epst[:, :], epsk[:, :], AX, OP.min))
    v_const = vC.n

    cast_stage_f = None
    cast_stage_b = None
    cast_busy = [0, 0]

    def make_cast_job(src_ap, dst_ap, sname):
        state = {"ld": 0, "cs": 0}

        def pload():
            i = state["ld"]
            if i >= DC:
                return
            state["ld"] += 1
            bi = i % 2

            @GE
            def _(g, i=i, bi=bi):
                if cast_busy[bi]:
                    g.wait_ge(vC.sem, cast_busy[bi])
                dm.put(g, "cwld", bi, cast_stage_f[bi][:], src_ap[i])

        def pcast():
            i = state["cs"]
            if i >= state["ld"] or i >= DC:
                return
            state["cs"] += 1
            bi = i % 2
            s_in = dm.get("cwld", bi)
            need = s_in.n

            @VE
            def _(v, bi=bi, s_in=s_in, need=need):
                v.wait_ge(s_in.sem, need)
                c = dm.get("cwst", bi)
                if c.n:
                    v.wait_ge(c.sem, c.n)
                vC.inc(v.tensor_copy(cast_stage_b[bi][:], cast_stage_f[bi][:]))
                cast_busy[bi] = vC.n
            v_cast = vC.n

            @GE
            def _(g, i=i, bi=bi, v_cast=v_cast):
                g.wait_ge(vC.sem, v_cast)
                dm.put(g, "cwst", bi, dst_ap[i], cast_stage_b[bi][:])
        return {"load": pload, "cast": pcast}

    s1g = nc.sbuf_tensor("xn2_fm", [P, DC * T], BF16, side="right")
    s1 = s1g.__enter__()

    p1 = ExitStack()
    xs = [p1.enter_context(nc.sbuf_tensor(f"xs{i}", [P, D], F32)) for i in range(4)]
    xn = [p1.enter_context(nc.sbuf_tensor(f"xn{i}", [P, D], F32)) for i in range(2)]
    xn2tm = [p1.enter_context(nc.sbuf_tensor(f"xn2tm{i}", [P, D], BF16))
             for i in range(2)]
    xq = p1.enter_context(nc.sbuf_tensor("xq", [P, D], U8))
    xdq = p1.enter_context(nc.sbuf_tensor("xdq", [P, D], F32))
    xhtm = p1.enter_context(nc.sbuf_tensor("xhtm", [P, D], BF16))
    xltm = p1.enter_context(nc.sbuf_tensor("xltm", [P, D], BF16))
    xhst = [p1.enter_context(nc.sbuf_tensor(f"xhst{i}", [P, D], BF16))
            for i in range(2)]
    xlst = [p1.enter_context(nc.sbuf_tensor(f"xlst{i}", [P, D], BF16))
            for i in range(2)]
    nbn = (D + 511) // 512
    bn6 = p1.enter_context(nc.sbuf_tensor("bn6", [P, 6 * nbn], F32))
    mvall = p1.enter_context(nc.sbuf_tensor("mvall", [P, 2 * TK], F32))
    uual = p1.enter_context(nc.sbuf_tensor("uual", [P, 2 * TK], F32))
    rral = p1.enter_context(nc.sbuf_tensor("rral", [P, 2 * TK], F32))
    nt1 = p1.enter_context(nc.sbuf_tensor("nt1", [P, 2 * TK], F32))
    nt2 = p1.enter_context(nc.sbuf_tensor("nt2", [P, 2 * TK], F32))
    zint = p1.enter_context(nc.sbuf_tensor("zint", [P, 2 * TK], mybir.dt.int32))
    c1al = p1.enter_context(nc.sbuf_tensor("c1al", [P, TK], F32))
    nb1al = p1.enter_context(nc.sbuf_tensor("nb1al", [P, TK], F32))
    nb2al = p1.enter_context(nc.sbuf_tensor("nb2al", [P, TK], F32))
    am = p1.enter_context(nc.sbuf_tensor("am", [P, DC], F32))
    sc_q = p1.enter_context(nc.sbuf_tensor("sc_q", [P, DC], F32))
    sc_iv = p1.enter_context(nc.sbuf_tensor("sc_iv", [P, DC], F32))
    cast_stage_f = [p1.enter_context(nc.sbuf_tensor(f"cstf{i}", [P, DC * P], F32))
                    for i in range(2)]
    cast_stage_b = [p1.enter_context(nc.sbuf_tensor(f"cstb{i}", [P, DC * P], BF16))
                    for i in range(2)]

    pump_wq = make_cast_job(w_r["q"], wbf_sc["q"], "cwq")

    xsa_free = [0, 0]
    for tc in range(TK):
        bi = tc % 2

        @SY
        def _(sy, tc=tc, bi=bi):
            if xsa_free[bi]:
                sy.wait_ge(vC.sem, xsa_free[bi])
            dm.put(sy, "p1ax", bi, xs[bi][:], x_d[tc * P:(tc + 1) * P, :])
        s_x = dm.get("p1ax", bi)
        need_x = s_x.n

        @VE
        def _(v, tc=tc, bi=bi, s_x=s_x, need_x=need_x):
            v.wait_ge(s_x.sem, need_x)
            v.wait_ge(vC.sem, vC.n)
            for b in range(nbn):
                w = min(512, D - b * 512)
                mm = v.bn_stats(bn6[:, 6 * b:6 * b + 6],
                                xs[bi][:, b * 512:b * 512 + w])
            vC.inc(mm)
            v.wait_ge(vC.sem, vC.n)
            vC.inc(v.bn_aggr(mvall[:, 2 * tc:2 * tc + 2], bn6[:, :]))
            xsa_free[bi] = vC.n

    mean_v = mvall[:].rearrange("p (t k) -> p t k", k=2)[:, :, 0:1]
    var_v = mvall[:].rearrange("p (t k) -> p t k", k=2)[:, :, 1:2]

    @VE
    def _(v):
        v.wait_ge(vC.sem, vC.n)
        u3 = uual[:, 0:TK].rearrange("p (t k) -> p t k", k=1)
        vC.inc(v.tensor_scalar(u3, var_v, EPS_LN, None, OP.add))
        v.wait_ge(vC.sem, vC.n)
        vC.inc(v.reciprocal(nt1[:, 0:TK], uual[:, 0:TK]))
        v.wait_ge(vC.sem, vC.n)
        vC.inc(v.tensor_tensor(nt2[:, 0:TK].rearrange("p (t k) -> p t k", k=1),
                               var_v,
                               nt1[:, 0:TK].rearrange("p (t k) -> p t k", k=1),
                               OP.mult))
        v.wait_ge(vC.sem, vC.n)
        vC.inc(v.tensor_tensor(uual[:, TK:2 * TK], nt2[:, 0:TK],
                               epst[:, 0:1].broadcast_to([P, TK]), OP.add))
        v.wait_ge(vC.sem, vC.n)
        vC.inc(v.tensor_copy(zint[:, :], uual[:].bitcast(mybir.dt.int32)))
        v.wait_ge(vC.sem, vC.n)
        vC.inc(v.tensor_scalar(zint[:, :], zint[:, :], 1, None,
                               OP.logical_shift_right))
        v.wait_ge(vC.sem, vC.n)
        vC.inc(v.tensor_scalar(zint[:, :], zint[:, :], -1, 0x5F3759DF,
                               OP.mult, OP.add))
        v.wait_ge(vC.sem, vC.n)
        vC.inc(v.tensor_copy(rral[:, :], zint[:].bitcast(F32)))
        for _ in range(5):
            v.wait_ge(vC.sem, vC.n)
            vC.inc(v.tensor_tensor(nt1[:, :], uual[:, :], rral[:, :], OP.mult))
            v.wait_ge(vC.sem, vC.n)
            vC.inc(v.tensor_tensor(nt2[:, :], nt1[:, :], rral[:, :], OP.mult))
            v.wait_ge(vC.sem, vC.n)
            vC.inc(v.tensor_scalar(nt2[:, :], nt2[:, :], -0.5, 1.5,
                                   OP.mult, OP.add))
            v.wait_ge(vC.sem, vC.n)
            vC.inc(v.tensor_tensor(rral[:, :], rral[:, :], nt2[:, :], OP.mult))
        v.wait_ge(vC.sem, vC.n)
        vC.inc(v.tensor_tensor(c1al[:, :], rral[:, 0:TK], rral[:, TK:2 * TK],
                               OP.mult))
        v.wait_ge(vC.sem, vC.n)
        v.tensor_tensor(nt1[:, 0:TK].rearrange("p (t k) -> p t k", k=1),
                        mean_v, rral[:, 0:TK].rearrange("p (t k) -> p t k", k=1),
                        OP.mult)
        vC.inc(v.tensor_tensor(
            nt2[:, 0:TK].rearrange("p (t k) -> p t k", k=1), mean_v,
            c1al[:].rearrange("p (t k) -> p t k", k=1), OP.mult))
        v.wait_ge(vC.sem, vC.n)
        v.tensor_scalar(nb1al[:, :], nt1[:, 0:TK], -1.0, None, OP.mult)
        vC.inc(v.tensor_scalar(nb2al[:, :], nt2[:, 0:TK], -1.0, None, OP.mult))
    v_scales = vC.n

    xn_free_v = [0, 0]
    xn2_used_t = [0, 0]
    bank_free = {}

    xs_free = [0, 0, 0, 0]
    pend_spill = [None]

    def emit_spill():
        if pend_spill[0] is None:
            return
        stc, sbi, sv = pend_spill[0]
        pend_spill[0] = None

        @SE
        def _(s, stc=stc, sbi=sbi, sv=sv):
            s.wait_ge(vC.sem, sv)
            dm.put(s, "p1st", sbi, xh_sc[stc], xhst[sbi][:])
            dm.put(s, "p1st", sbi, xl_sc[stc], xlst[sbi][:])

    p1a_guard = list(xsa_free)
    for tc in range(TK):
        bi = tc % 2
        bx = tc % 4
        pump_wq["load"]()

        @SY
        def _(sy, tc=tc, bx=bx):
            if xs_free[bx]:
                sy.wait_ge(sC.sem, xs_free[bx])
            elif bx < 2 and p1a_guard[bx]:
                sy.wait_ge(vC.sem, p1a_guard[bx])
            dm.put(sy, "p1x", bx % 2, xs[bx][:], x_d[tc * P:(tc + 1) * P, :])
        s_x = dm.get("p1x", bx % 2)
        need_x = s_x.n

        @SE
        def _(s, tc=tc, bi=bi, bx=bx, s_x=s_x, need_x=need_x):
            s.wait_ge(s_x.sem, need_x)
            s.wait_ge(vC.sem, v_scales)
            if xn_free_v[bi]:
                s.wait_ge(vC.sem, xn_free_v[bi])
            if xn2_used_t[bi]:
                s.wait_ge(tC.sem, xn2_used_t[bi])
            s.activation(xn[bi][:, :], xs[bx][:, :], ACT.Identity,
                         bias=nb1al[:, tc:tc + 1], scale=rral[:, tc:tc + 1])
            sC.inc(s.activation(xn2tm[bi][:, :], xs[bx][:, :], ACT.Identity,
                                bias=nb2al[:, tc:tc + 1],
                                scale=c1al[:, tc:tc + 1]))
            xs_free[bx] = sC.n
        s_xn = sC.n
        emit_spill()

        n_grp = (DC + 3) // 4

        for grp in range(n_grp):
            dc0 = grp * 4
            ndc = min(4, DC - dc0)
            bank = 4 + grp % 4

            @TE
            def _(t, dc0=dc0, ndc=ndc, bank=bank, bi=bi, s_xn=s_xn,
                  last=(grp == n_grp - 1)):
                s_const.wait(t)
                t.wait_ge(sC.sem, s_xn)
                fv = bank_free.get(bank, 0)
                if fv:
                    t.wait_ge(sC.sem if fv[0] == "s" else vC.sem, fv[1])
                for k in range(ndc):
                    mm = t.matmul(ps[bank][:, k * P:(k + 1) * P],
                                  xn2tm[bi][:, (dc0 + k) * P:(dc0 + k + 1) * P],
                                  ident[:, :], start=True, stop=True)
                n = tC.inc(mm)
                if last:
                    xn2_used_t[bi] = n
            t_tr = tC.n

            @VE
            def _(v, dc0=dc0, ndc=ndc, bank=bank, tc=tc, t_tr=t_tr):
                v.wait_ge(tC.sem, t_tr)
                dst = s1[:].rearrange("p (c t) -> p c t", c=DC)[
                    :, dc0:dc0 + ndc, tc * P:(tc + 1) * P]
                src_ap = ps[bank][:].rearrange("p (c k) -> p c k", c=4)[
                    :, 0:ndc, :]
                vC.inc(v.tensor_copy(dst, src_ap))
                bank_free[bank] = ("v", vC.n)

        @VE
        def _(v, tc=tc, bi=bi, s_xn=s_xn):
            v.wait_ge(sC.sem, s_xn)
            v.wait_ge(vC.sem, v_const)
            xn3 = xn[bi][:].rearrange("p (c k) -> p c k", c=DC)
            vC.inc(v.tensor_reduce(am[:, :], xn3, AX, OP.max,
                                   apply_absolute_value=True))
            xn_free_v[bi] = vC.n
            v.wait_ge(vC.sem, vC.n)
            vC.inc(v.tensor_scalar(am[:, :], am[:, :], 1.0e-4, None, OP.max))
            v.wait_ge(vC.sem, vC.n)
            vC.inc(v.reciprocal(sc_q[:, :], am[:, :]))
            v.wait_ge(vC.sem, vC.n)
            v.tensor_scalar(sc_q[:, :], sc_q[:, :], FP8_MAX, None, OP.mult)
            vC.inc(v.tensor_scalar(sc_iv[:, :], am[:, :], 1.0 / FP8_MAX, None,
                                   OP.mult))
            v.wait_ge(vC.sem, vC.n)
            scb = sc_q[:].unsqueeze(2).broadcast_to([P, DC, P])
            vC.inc(v.tensor_tensor(
                xq[:].bitcast(F8).rearrange("p (c k) -> p c k", c=DC),
                xn3, scb, OP.mult))
            v.wait_ge(vC.sem, vC.n)
            scivb = sc_iv[:].unsqueeze(2).broadcast_to([P, DC, P])
            vC.inc(v.tensor_tensor(
                xdq[:].rearrange("p (c k) -> p c k", c=DC),
                xq[:].bitcast(F8).rearrange("p (c k) -> p c k", c=DC),
                scivb, OP.mult))
            v.wait_ge(vC.sem, vC.n)
            vC.inc(v.tensor_copy(xhtm[:, :], xdq[:, :]))
            v.wait_ge(vC.sem, vC.n)
            vC.inc(v.tensor_tensor(xltm[:, :], xdq[:, :], xhtm[:, :],
                                   OP.subtract))
        v_tm = vC.n

        for si, (src_t, kind) in enumerate([(xhtm, "xh"), (xltm, "xl")]):
            for grp in range(n_grp):
                dc0 = grp * 4
                ndc = min(4, DC - dc0)
                bank = (si * n_grp + grp) % 4

                @TE
                def _(t, src_t=src_t, dc0=dc0, ndc=ndc, bank=bank, v_tm=v_tm):
                    t.wait_ge(vC.sem, v_tm)
                    fv = bank_free.get(bank, 0)
                    if fv:
                        t.wait_ge(sC.sem if fv[0] == "s" else vC.sem, fv[1])
                    for k in range(ndc):
                        mm = t.matmul(ps[bank][:, k * P:(k + 1) * P],
                                      src_t[:, (dc0 + k) * P:(dc0 + k + 1) * P],
                                      ident[:, :], start=True, stop=True)
                    tC.inc(mm)
                t_tr = tC.n

                @VE
                def _(v, kind=kind, dc0=dc0, ndc=ndc, bank=bank, bi=bi,
                      t_tr=t_tr):
                    v.wait_ge(tC.sem, t_tr)
                    sst = dm.get("p1st", bi)
                    sst.wait(v)
                    dst = (xhst if kind == "xh" else xlst)[bi][
                        :, dc0 * P:(dc0 + ndc) * P]
                    vC.inc(v.tensor_copy(dst, ps[bank][:, 0:ndc * P]))
                    bank_free[bank] = ("v", vC.n)

        pend_spill[0] = (tc, bi, vC.n)
        pump_wq["cast"]()

    emit_spill()
    for _i in range(DC):
        pump_wq["load"]()
        pump_wq["cast"]()

    if debug:
        @SY
        def _(sy):
            for c in (vC, sC, tC):
                sy.wait_ge(c.sem, c.n)
            dm.put(sy, "dbg", 0, xn2_dbg[:, :], s1[:])

    join(SY, VE, SE, TE, GE)
    p1.close()

    p2a = ExitStack()
    kfm = p2a.enter_context(nc.sbuf_tensor("kfm", [P, DC * T], BF16))
    p2 = ExitStack()
    wbuf = [p2.enter_context(nc.sbuf_tensor(f"wbuf{i}", [P, DC * P], BF16))
            for i in range(2)]
    pstage = [p2.enter_context(nc.sbuf_tensor(f"pstage{i}", [P, TW], BF16))
              for i in range(2)]
    cast_stage_f = [p2.enter_context(nc.sbuf_tensor(f"c2f{i}", [P, DC * P], F32))
                    for i in range(2)]
    cast_stage_b = [p2.enter_context(nc.sbuf_tensor(f"c2b{i}", [P, DC * P], BF16))
                    for i in range(2)]
    wfband = [p2.enter_context(nc.sbuf_tensor(f"wfb{i}", [P, D], F32))
              for i in range(2)]
    wdq = p2.enter_context(nc.sbuf_tensor("wdq", [P, D], F32))
    wq8 = p2.enter_context(nc.sbuf_tensor("wq8", [P, D], U8))
    whb = p2.enter_context(nc.sbuf_tensor("whb", [P, D], BF16))
    wlb = p2.enter_context(nc.sbuf_tensor("wlb", [P, D], BF16))
    NB = DC * DC
    wam_all = p2.enter_context(nc.sbuf_tensor("wam_all", [P, NB], F32))
    wtmp = p2.enter_context(nc.sbuf_tensor("wtmp", [P, NB], F32))
    wsc_all = p2.enter_context(nc.sbuf_tensor("wsc_all", [P, NB], F32))
    wsiv_all = p2.enter_context(nc.sbuf_tensor("wsiv_all", [P, NB], F32))

    wfa = {"dc": 0, "free": [0, 0]}

    def pump_wf_amax():
        dc = wfa["dc"]
        if dc >= DC:
            return
        wfa["dc"] += 1
        bi = dc % 2

        @SY
        def _(sy, dc=dc, bi=bi):
            if wfa["free"][bi]:
                sy.wait_ge(vC.sem, wfa["free"][bi])
            dm.put(sy, "wfa", bi, wfband[bi][:], wf_t[dc * P:(dc + 1) * P, :])
        s_wf = dm.get("wfa", bi)
        need = s_wf.n

        @VE
        def _(v, dc=dc, bi=bi, s_wf=s_wf, need=need):
            v.wait_ge(s_wf.sem, need)
            v.wait_ge(vC.sem, vC.n)
            w3 = wfband[bi][:].rearrange("p (c k) -> p c k", c=DC)
            vC.inc(v.tensor_reduce(wam_all[:, dc * DC:(dc + 1) * DC], w3, AX,
                                   OP.max, apply_absolute_value=True))
            wfa["free"][bi] = vC.n

    def emit_wf_tree():
        sz = P // 2
        step = 0
        prev_v = vC.n
        while sz >= 1:
            @GE
            def _(sy, sz=sz, prev_v=prev_v, step=step):
                sy.wait_ge(vC.sem, prev_v)
                dm.put(sy, "tree", step % 2, wtmp[0:sz, :], wam_all[sz:2 * sz, :])
            s_t = dm.get("tree", step % 2)
            need_t = s_t.n

            @VE
            def _(v, sz=sz, s_t=s_t, need_t=need_t):
                v.wait_ge(s_t.sem, need_t)
                v.wait_ge(vC.sem, vC.n)
                vC.inc(v.tensor_tensor(wam_all[0:sz, :], wam_all[0:sz, :],
                                       wtmp[0:sz, :], OP.max))
            prev_v = vC.n
            sz //= 2
            step += 1
        k = 1
        step = 0
        while k < P:
            @GE
            def _(sy, k=k, prev_v=prev_v, step=step):
                if step == 0:
                    sy.wait_ge(vC.sem, prev_v)
                else:
                    c = dm.get("bcst", (step - 1) % 2)
                    sy.wait_ge(c.sem, c.n)
                dm.put(sy, "bcst", step % 2, wam_all[k:min(2 * k, P), :],
                       wam_all[0:min(k, P - k), :])
            k *= 2
            step += 1
        s_bc = dm.get("bcst", (step - 1) % 2)
        need_bc = s_bc.n

        @VE
        def _(v, s_bc=s_bc, need_bc=need_bc):
            v.wait_ge(s_bc.sem, need_bc)
            v.wait_ge(vC.sem, v_const)
            vC.inc(v.tensor_scalar(wam_all[:, :], wam_all[:, :], 1.0e-4, None,
                                   OP.max))
            v.wait_ge(vC.sem, vC.n)
            vC.inc(v.reciprocal(wsc_all[:, :], wam_all[:, :]))
            v.wait_ge(vC.sem, vC.n)
            v.tensor_scalar(wsc_all[:, :], wsc_all[:, :], FP8_MAX, None, OP.mult)
            vC.inc(v.tensor_scalar(wsiv_all[:, :], wam_all[:, :], 1.0 / FP8_MAX,
                                   None, OP.mult))

    wfq = {"dc": 0, "free": [0, 0], "st": 0}

    def pump_wf_quant():
        dc = wfq["dc"]
        if dc >= DC:
            return
        wfq["dc"] += 1
        bi = dc % 2

        @SY
        def _(sy, dc=dc, bi=bi):
            if wfq["free"][bi]:
                sy.wait_ge(vC.sem, wfq["free"][bi])
            dm.put(sy, "wfq", bi, wfband[bi][:], wf_t[dc * P:(dc + 1) * P, :])
        s_wf = dm.get("wfq", bi)
        need = s_wf.n

        @VE
        def _(v, dc=dc, bi=bi, s_wf=s_wf, need=need):
            v.wait_ge(s_wf.sem, need)
            v.wait_ge(vC.sem, vC.n)
            c = dm.get("whlst", 0)
            if c.n:
                v.wait_ge(c.sem, c.n)
            w3 = wfband[bi][:].rearrange("p (c k) -> p c k", c=DC)
            wscb = wsc_all[:, dc * DC:(dc + 1) * DC].unsqueeze(2) \
                .broadcast_to([P, DC, P])
            vC.inc(v.tensor_tensor(
                wq8[:].bitcast(F8).rearrange("p (c k) -> p c k", c=DC),
                w3, wscb, OP.mult))
            wfq["free"][bi] = vC.n
            v.wait_ge(vC.sem, vC.n)
            wsivb = wsiv_all[:, dc * DC:(dc + 1) * DC].unsqueeze(2) \
                .broadcast_to([P, DC, P])
            vC.inc(v.tensor_tensor(
                wdq[:].rearrange("p (c k) -> p c k", c=DC),
                wq8[:].bitcast(F8).rearrange("p (c k) -> p c k", c=DC),
                wsivb, OP.mult))
            v.wait_ge(vC.sem, vC.n)
            vC.inc(v.tensor_copy(whb[:, :], wdq[:, :]))
            v.wait_ge(vC.sem, vC.n)
            vC.inc(v.tensor_tensor(wlb[:, :], wdq[:, :], whb[:, :], OP.subtract))
        v_whl = vC.n

        @SE
        def _(s, dc=dc, v_whl=v_whl):
            s.wait_ge(vC.sem, v_whl)
            dm.put(s, "whlst", 0, wh_sc[dc], whb[:])
            dm.put(s, "whlst", 0, wl_sc[dc], wlb[:])

    pump_casts = {
        "q": make_cast_job(w_r["k"], wbf_sc["k"], "cwk"),
        "k": make_cast_job(w_r["g"], wbf_sc["g"], "cwg"),
        "g": make_cast_job(w_r["v"], wbf_sc["v"], "cwv"),
        "v": make_cast_job(
            wp_t.rearrange("(c p) d -> c p d", p=P), wpbf_sc, "cwp"),
    }

    wb_free = [0, 0]
    grp_free = {}
    w_cfg = [("q", ACT.Silu, q_sc), ("k", ACT.Silu, None),
             ("g", ACT.Sigmoid, g_sc), ("v", ACT.Silu, v_sc)]
    for wi, (wname, actf, dstsc) in enumerate(w_cfg):
        for oc in range(DC):
            bi = oc % 2
            pump_casts[wname]["cast"]()
            pump_casts[wname]["load"]()
            if wname == "q":
                pump_wf_amax()
            elif wname in ("k", "g") and (oc % 2 == 0):
                pump_wf_quant()

            @SY
            def _(sy, wname=wname, oc=oc, bi=bi):
                if wb_free[bi]:
                    sy.wait_ge(tC.sem, wb_free[bi])
                if oc == 0:
                    for par in (0, 1):
                        dm.get("cwst", par).wait(sy)
                dm.put(sy, "p2w", bi, wbuf[bi][:], wbf_sc[wname][oc])
            s_w = dm.get("p2w", bi)
            need_w = s_w.n

            banks = tuple((oc % 2) * 4 + t for t in range(NTT))

            @TE
            def _(t, oc=oc, bi=bi, banks=banks, s_w=s_w, need_w=need_w):
                t.wait_ge(s_w.sem, need_w)
                fs = grp_free.get(banks, 0)
                if fs:
                    t.wait_ge(sC.sem, fs)
                for dc in range(DC):
                    for tt in range(NTT):
                        mm = t.matmul(
                            ps[banks[tt]][:, 0:TW],
                            wbuf[bi][:, dc * P:(dc + 1) * P],
                            s1[:, dc * T + tt * TW: dc * T + (tt + 1) * TW],
                            start=(dc == 0), stop=(dc == DC - 1))
                wb_free[bi] = tC.inc(mm)
            t_proj = tC.n

            if wname == "k":
                @SE
                def _(s, oc=oc, banks=banks, t_proj=t_proj):
                    s.wait_ge(tC.sem, t_proj)
                    for tt in range(NTT):
                        mm = s.activation(
                            kfm[:, oc * T + tt * TW: oc * T + (tt + 1) * TW],
                            ps[banks[tt]][:, 0:TW], ACT.Silu)
                    grp_free[banks] = sC.inc(mm)
            else:
                for tt in range(NTT):
                    sbi = (oc * NTT + tt) % 2

                    @SE
                    def _(s, oc=oc, tt=tt, sbi=sbi, banks=banks, actf=actf,
                          dstsc=dstsc, t_proj=t_proj):
                        if tt == 0:
                            s.wait_ge(tC.sem, t_proj)
                        sst = dm.get("p2st", sbi)
                        sst.wait(s)
                        sC.inc(s.activation(pstage[sbi][:, :],
                                            ps[banks[tt]][:, 0:TW], actf))
                        s.wait_ge(sC.sem, sC.n)
                        dm.put(s, "p2st", sbi, dstsc[tt, oc], pstage[sbi][:])
                grp_free[banks] = sC.n
        pump_casts[wname]["cast"]()
        if wname == "q":
            emit_wf_tree()
            wfq["free"] = list(wfa["free"])

    join(SY, VE, SE, TE, GE)
    s1g.__exit__(None, None, None)
    p2.close()

    p3 = ExitStack()
    vtm = p3.enter_context(nc.sbuf_tensor("vtm", [P, TK * D], BF16))
    qsl = [p3.enter_context(nc.sbuf_tensor(f"qsl{i}", [P, DC * TW], BF16,
                                           side="right")) for i in range(2)]
    stsb = p3.enter_context(nc.sbuf_tensor("stsb", [P, TK * TW], BF16,
                                           side="right"))
    ostage = [p3.enter_context(nc.sbuf_tensor(f"ostage{i}", [P, TW], BF16,
                                              side="right")) for i in range(4)]
    gbuf = [p3.enter_context(nc.sbuf_tensor(f"gbuf{i}", [P, TW], BF16,
                                            side="right")) for i in range(4)]
    vbst = [p3.enter_context(nc.sbuf_tensor(f"vbst{i}", [P, DC * P], BF16,
                                            side="right")) for i in range(2)]

    g_free = [0, 0, 0, 0]
    sbank_free = {}
    obank_free = {}
    vb_free = [0, 0]
    qsl_mm = [0, 0]
    for it in range(NTT):
        jmax = it * IT + IT - 1
        qpar = it % 2

        @SY
        def _(sy, it=it, qpar=qpar):
            if qsl_mm[qpar]:
                sy.wait_ge(tC.sem, qsl_mm[qpar])
            dm.put(sy, "p3q", qpar,
                   qsl[qpar][:].rearrange("p (c w) -> p c w", c=DC),
                   q_sc[it].rearrange("c p w -> p c w"))
        s_q = dm.get("p3q", qpar)
        need_q = s_q.n

        for J in range(jmax + 1):
            off = max(0, (J - it * IT)) * P
            bank = J % 2

            @TE
            def _(t, J=J, off=off, bank=bank, s_q=s_q, need_q=need_q,
                  qpar=qpar, jmax=jmax):
                if J == 0:
                    t.wait_ge(s_q.sem, need_q)
                fv = sbank_free.get(("s", bank), 0)
                if fv:
                    t.wait_ge(vC.sem, fv)
                for oc in range(DC):
                    mm = t.matmul(
                        ps[bank][:, off:TW],
                        kfm[:, oc * T + J * P: oc * T + (J + 1) * P],
                        qsl[qpar][:, oc * TW + off: (oc + 1) * TW],
                        start=(oc == 0), stop=(oc == DC - 1))
                n = tC.inc(mm)
                if J == jmax:
                    qsl_mm[qpar] = n
            t_s = tC.n

            @VE
            def _(v, it=it, J=J, off=off, bank=bank, t_s=t_s):
                v.wait_ge(tC.sem, t_s)
                s_const.wait(v)
                diag = J >= it * IT
                if diag:
                    mm = v.tensor_tensor(stsb[:, J * TW + off: J * TW + off + P],
                                         ps[bank][:, off:off + P], trimask[:, :],
                                         OP.mult)
                    if TW - off - P > 0:
                        mm = v.tensor_copy(stsb[:, J * TW + off + P: (J + 1) * TW],
                                           ps[bank][:, off + P:TW])
                else:
                    mm = v.tensor_copy(stsb[:, J * TW: (J + 1) * TW],
                                       ps[bank][:, 0:TW])
                vC.inc(mm)
                sbank_free[("s", bank)] = vC.n
        st_all = vC.n

        for jb in range(IT):
            J = it * IT + jb
            vbi = jb % 2

            @GE
            def _(sy, it=it, jb=jb, vbi=vbi):
                if vb_free[vbi]:
                    sy.wait_ge(tC.sem, vb_free[vbi])
                dm.put(sy, "p3vb", vbi,
                       vbst[vbi][:].rearrange("p (c w) -> p c w", c=DC),
                       v_sc[it].rearrange("c p w -> p c w")[
                           :, :, jb * P:(jb + 1) * P])
            s_vb = dm.get("p3vb", vbi)
            need_vb = s_vb.n

            n_vgrp = (DC + 3) // 4
            for grp in range(n_vgrp):
                bank = 6 + grp % 2
                nk = min(4, DC - grp * 4)

                @TE
                def _(t, vbi=vbi, grp=grp, bank=bank, nk=nk, s_vb=s_vb,
                      need_vb=need_vb, last=(grp == n_vgrp - 1)):
                    t.wait_ge(s_vb.sem, need_vb)
                    s_const.wait(t)
                    fs = sbank_free.get(bank, 0)
                    if fs:
                        t.wait_ge(vC.sem, fs)
                    for k in range(nk):
                        oc = grp * 4 + k
                        mm = t.matmul(ps[bank][:, k * P:(k + 1) * P],
                                      vbst[vbi][:, oc * P:(oc + 1) * P],
                                      ident[:, :], start=True, stop=True)
                    n = tC.inc(mm)
                    if last:
                        vb_free[vbi] = n
                t_vtr = tC.n

                @VE
                def _(v, J=J, grp=grp, bank=bank, nk=nk, t_vtr=t_vtr):
                    v.wait_ge(tC.sem, t_vtr)
                    vC.inc(v.tensor_copy(
                        vtm[:, J * D + grp * 4 * P: J * D + (grp * 4 + nk) * P],
                        ps[bank][:, 0:nk * P]))
                    sbank_free[bank] = vC.n
        vtm_ready = vC.n

        for ec in range(DC):
            bank = 2 + ec % 3
            gbi = ec % 4
            obi = ec % 4

            @GE
            def _(sy, it=it, ec=ec, gbi=gbi):
                if g_free[gbi]:
                    sy.wait_ge(vC.sem, g_free[gbi])
                dm.put(sy, "p3g", gbi, gbuf[gbi][:], g_sc[it, ec])
            s_g = dm.get("p3g", gbi)
            need_g = s_g.n

            @TE
            def _(t, it=it, ec=ec, bank=bank, jmax=jmax, st_all=st_all,
                  vtm_ready=vtm_ready):
                t.wait_ge(vC.sem, st_all)
                if ec == 0:
                    t.wait_ge(vC.sem, vtm_ready)
                fv = obank_free.get(bank, 0)
                if fv:
                    t.wait_ge(vC.sem, fv)
                for J in range(jmax + 1):
                    off = max(0, (J - it * IT)) * P
                    mm = t.matmul(
                        ps[bank][:, off:TW],
                        vtm[:, J * D + ec * P: J * D + (ec + 1) * P],
                        stsb[:, J * TW + off: (J + 1) * TW],
                        start=(J == 0), stop=(J == jmax))
                tC.inc(mm)
            t_sv = tC.n

            @VE
            def _(v, bank=bank, gbi=gbi, obi=obi, t_sv=t_sv, s_g=s_g,
                  need_g=need_g):
                v.wait_ge(tC.sem, t_sv)
                v.wait_ge(s_g.sem, need_g)
                sost = dm.get("p3o", obi)
                sost.wait(v)
                n = vC.inc(v.tensor_tensor(ostage[obi][:, :], ps[bank][:, 0:TW],
                                           gbuf[gbi][:, :], OP.mult))
                obank_free[bank] = n
                g_free[gbi] = n
            v_o = vC.n

            @SE
            def _(s, it=it, ec=ec, obi=obi, v_o=v_o):
                s.wait_ge(vC.sem, v_o)
                dm.put(s, "p3o", obi, o_sc[it, ec], ostage[obi][:])

    s3_done = qsl_mm[(NTT - 1) % 2]

    @SY
    def _(sy, s3_done=s3_done):
        sy.wait_ge(tC.sem, s3_done)
        dm.get("whlst", 0).wait(sy)
        dm.put(sy, "whpre", 0,
               kfm[:].rearrange("p (c w) -> p c w", c=DC),
               wh_sc.rearrange("c p w -> p c w"))

    join(SY, VE, SE, TE, GE)
    p3.close()

    pf = ExitStack()
    whs = kfm
    wls = pf.enter_context(nc.sbuf_tensor("wls", [P, DC * D], BF16))
    xhf = [pf.enter_context(nc.sbuf_tensor(f"xhf{i}", [P, D], BF16))
           for i in range(2)]
    xlf = [pf.enter_context(nc.sbuf_tensor(f"xlf{i}", [P, D], BF16))
           for i in range(2)]
    fst = [pf.enter_context(nc.sbuf_tensor(f"fst{i}", [P, D], U16))
           for i in range(2)]

    @SE
    def _(s):
        dm.put(s, "whl_in", 0,
               wls[:].rearrange("p (c w) -> p c w", c=DC),
               wl_sc.rearrange("c p w -> p c w"))
    s_whl = dm.get("whl_in", 0)
    need_whl = s_whl.n

    xf_free = [0, 0]
    fbank_free = {}
    for tc in range(TK):
        bi = tc % 2

        @SY
        def _(sy, tc=tc, bi=bi):
            if xf_free[bi]:
                sy.wait_ge(tC.sem, xf_free[bi])
            dm.put(sy, "pfx", bi, xhf[bi][:], xh_sc[tc])
            dm.put(sy, "pfx", bi, xlf[bi][:], xl_sc[tc])
        s_xf = dm.get("pfx", bi)
        need_xf = s_xf.n

        banks = tuple((tc % 2) * 4 + t for t in range(NOT_))

        @TE
        def _(t, bi=bi, banks=banks, s_xf=s_xf, need_xf=need_xf,
              first=(tc == 0)):
            t.wait_ge(s_xf.sem, need_xf)
            if fbank_free.get(banks, 0):
                t.wait_ge(vC.sem, fbank_free[banks])
            chains = [(xhf[bi], whs), (xlf[bi], whs), (xhf[bi], wls)]
            nch = len(chains)
            for ci, (lhs, rhs) in enumerate(chains):
                if first and ci == nch - 1:
                    t.wait_ge(s_whl.sem, need_whl)
                for dc in range(DC):
                    for ot in range(NOT_):
                        mm = t.matmul(
                            ps[banks[ot]][:, 0:OW],
                            lhs[:, dc * P:(dc + 1) * P],
                            rhs[:, dc * D + ot * OW: dc * D + (ot + 1) * OW],
                            start=(ci == 0 and dc == 0),
                            stop=(ci == nch - 1 and dc == DC - 1))
            xf_free[bi] = tC.inc(mm)
        t_ffn = tC.n

        @VE
        def _(v, bi=bi, banks=banks, t_ffn=t_ffn):
            v.wait_ge(tC.sem, t_ffn)
            sfst = dm.get("pfst", bi)
            sfst.wait(v)
            for ot in range(NOT_):
                mm = v.tensor_copy(
                    fst[bi][:].bitcast(BF16)[:, ot * OW:(ot + 1) * OW],
                    ps[banks[ot]][:, 0:OW])
            fbank_free[banks] = vC.inc(mm)
        v_ffn = vC.n

        @SE
        def _(s, tc=tc, bi=bi, v_ffn=v_ffn):
            s.wait_ge(vC.sem, v_ffn)
            dm.put(s, "pfst", bi, ffn_sc[tc], fst[bi][:])

    join(SY, VE, SE, TE, GE)
    pf.close()
    p2a.close()

    p4 = ExitStack()
    wps = p4.enter_context(nc.sbuf_tensor("wps", [P, DC * D], BF16))
    osl = [p4.enter_context(nc.sbuf_tensor(f"osl{i}", [P, DC * TW], BF16))
           for i in range(2)]
    xrow = [p4.enter_context(nc.sbuf_tensor(f"xrow{i}", [P, D], F32))
            for i in range(2)]
    frow = [p4.enter_context(nc.sbuf_tensor(f"frow{i}", [P, D], U16))
            for i in range(2)]
    tmpv = p4.enter_context(nc.sbuf_tensor("tmpv", [P, D], F32))
    orow = [p4.enter_context(nc.sbuf_tensor(f"orow{i}", [P, D], F32))
            for i in range(2)]

    @SY
    def _(sy):
        dm.put(sy, "p4wp", 0,
               wps[:].rearrange("p (c w) -> p c w", c=DC),
               wpbf_sc.rearrange("c p w -> p c w"))
    s_wp = dm.get("p4wp", 0)
    need_wp = s_wp.n

    row_free = [0, 0]
    pbank_free = {}
    osl_mm = [0, 0]
    for it in range(NTT):
        opar = it % 2

        @SY
        def _(sy, it=it, opar=opar):
            if osl_mm[opar]:
                sy.wait_ge(tC.sem, osl_mm[opar])
            dm.put(sy, "p4o", opar,
                   osl[opar][:].rearrange("p (c w) -> p c w", c=DC),
                   o_sc[it].rearrange("c p w -> p c w"))
        s_o = dm.get("p4o", opar)
        need_o = s_o.n

        for icl in range(IT):
            ic = it * IT + icl
            bi = ic % 2
            banks = tuple((ic % 2) * 4 + t for t in range(NOT_))

            @SY
            def _(sy, ic=ic, bi=bi):
                if row_free[bi]:
                    sy.wait_ge(vC.sem, row_free[bi])
                dm.put(sy, "p4r", bi, xrow[bi][:], x_d[ic * P:(ic + 1) * P, :])
                dm.put(sy, "p4r", bi, frow[bi][:], ffn_sc[ic])
            s_r = dm.get("p4r", bi)
            need_rows = s_r.n

            @TE
            def _(t, icl=icl, banks=banks, first=(it == 0 and icl == 0),
                  s_o=s_o, need_o=need_o, opar=opar, last=(icl == IT - 1)):
                if first:
                    t.wait_ge(s_wp.sem, need_wp)
                t.wait_ge(s_o.sem, need_o)
                if pbank_free.get(banks, 0):
                    t.wait_ge(vC.sem, pbank_free[banks])
                for ec in range(DC):
                    for ot in range(NOT_):
                        mm = t.matmul(
                            ps[banks[ot]][:, 0:OW],
                            osl[opar][:, ec * TW + icl * P: ec * TW + (icl + 1) * P],
                            wps[:, ec * D + ot * OW: ec * D + (ot + 1) * OW],
                            start=(ec == 0), stop=(ec == DC - 1))
                n = tC.inc(mm)
                if last:
                    osl_mm[opar] = n
            t_p4 = tC.n

            @VE
            def _(v, bi=bi, banks=banks, t_p4=t_p4, s_r=s_r,
                  need_rows=need_rows):
                v.wait_ge(tC.sem, t_p4)
                v.wait_ge(s_r.sem, need_rows)
                v.wait_ge(vC.sem, vC.n)
                sorow = dm.get("p4out", bi)
                sorow.wait(v)
                for ot in range(NOT_):
                    sl = slice(ot * OW, (ot + 1) * OW)
                    mm = v.scalar_tensor_tensor(tmpv[:, sl], ps[banks[ot]][:, 0:OW],
                                                1.0, xrow[bi][:, sl],
                                                OP.mult, OP.add)
                vC.inc(mm)
                v.wait_ge(vC.sem, vC.n)
                for ot in range(NOT_):
                    sl = slice(ot * OW, (ot + 1) * OW)
                    mm = v.tensor_tensor(orow[bi][:, sl], tmpv[:, sl],
                                         frow[bi][:].bitcast(BF16)[:, sl], OP.add)
                n = vC.inc(mm)
                row_free[bi] = n
                pbank_free[banks] = n
            v_p4 = vC.n

            @SE
            def _(s, ic=ic, bi=bi, v_p4=v_p4):
                s.wait_ge(vC.sem, v_p4)
                dm.put(s, "p4out", bi, out_d[ic * P:(ic + 1) * P, :], orow[bi][:])

    @SY
    def _(sy):
        for c in dm.all():
            c.wait(sy)

    p4.close()
    es.close()
    return nc


def make_inputs(x_c, Wq, Wk, Wv, Wg, Wp, Wf, DC=16):
    bf = ml_dtypes.bfloat16

    def reblock(W):
        W4 = np.asarray(W, np.float32).reshape(DC, P, DC, P)
        return np.ascontiguousarray(W4.transpose(0, 3, 2, 1)
                                    ).reshape(DC, P, DC * P)

    return {
        "x": np.ascontiguousarray(np.asarray(x_c, np.float32)),
        "wq_r": reblock(Wq), "wk_r": reblock(Wk),
        "wg_r": reblock(Wg), "wv_r": reblock(Wv),
        "wp_t": np.ascontiguousarray(np.asarray(Wp, np.float32).T),
        "wf_t": np.ascontiguousarray(np.asarray(Wf, np.float32).T),
        "ident": np.eye(P, dtype=np.float32).astype(bf),
        "trimask": np.triu(np.ones((P, P), np.float32)).astype(bf),
    }


def kernel(L0_attn_x, L0_attn_Wq, L0_attn_Wk, L0_attn_Wv, L0_attn_Wg,
           L0_attn_Wp, L0_ffn_weight):
    x = np.asarray(L0_attn_x, np.float32)
    B = x.shape[0]
    nc = build_program(TK=16, DC=16, debug=False)
    base = make_inputs(x[0], L0_attn_Wq, L0_attn_Wk, L0_attn_Wv, L0_attn_Wg,
                       L0_attn_Wp, L0_ffn_weight)
    in_maps = []
    for c in range(B):
        m = dict(base)
        m["x"] = np.ascontiguousarray(x[c])
        in_maps.append(m)
    res = run_bass_kernel_spmd(nc, in_maps, core_ids=list(range(B)))
    out = np.stack([np.asarray(r["out"], np.float32) for r in res.results], axis=0)
    return out
